# version 27
# baseline (speedup 1.0000x reference)
import numpy as np
import ml_dtypes
import concourse.bass as bass
import concourse.mybir as mybir
from concourse.bass_utils import run_bass_kernel_spmd

F32 = mybir.dt.float32
BF16 = mybir.dt.bfloat16
AF = mybir.ActivationFunctionType
ALU = mybir.AluOpType
AX = mybir.AxisListType

M32 = 12582912.0
M64 = 6755399441055744.0

T = 2048
C = 1024
NH = 16
KVL = 256
QL = 256
INNER = 4096
NSEQ = 2
NT = T // 128
NTB = T // 512
TTOT = NSEQ * T
NTT = TTOT // 128


class S:
    __slots__ = ("h", "n")
    def __init__(self, h):
        self.h = h
        self.n = 0


class E:
    def __init__(self, real):
        self.real = real

    def op(self, meth, *a, inc=None, inc16=None, **k):
        ins = None
        if self.real is not None:
            ins = getattr(self.real, meth)(*a, **k)
        if inc is not None:
            inc.n += 1
            if ins is not None:
                ins.then_inc(inc.h, 1)
        if inc16 is not None:
            inc16.n += 16
            if ins is not None:
                ins.then_inc(inc16.h, 16)
        return ins

    def wait(self, sem, val=None):
        v = sem.n if val is None else val
        if self.real is not None:
            self.real.wait_ge(sem.h, v)


class Ctx:
    pass


def _alloc_sbuf(nc):
    b = {}
    BASE = 17408
    def at(name, shape, dtype, off):
        off = (off + 31) & ~31
        b[name] = nc.alloc_sbuf_tensor_at(name, shape, dtype, offset=BASE + off)
        return off + shape[1] * mybir.dt.size(dtype)
    o = 0
    o = at("ident", [128, 128], BF16, o)
    o = at("maskW", [128, 896], BF16, o)
    o = at("ones_bf", [128, 128], BF16, o)
    o = at("onesr_f", [1, 128], F32, o)
    o = at("ones128_f", [128, 1], F32, o)
    o = at("dD", [128, 1], F32, o)
    o = at("mqt", [128, 1], F32, o)
    o = at("nmqt", [128, 1], F32, o)
    o = at("consts", [128, 8], F32, o)
    for nm in ("st_a", "st_b", "st_c", "st_d", "st_e", "st_f", "st_g", "st_h"):
        o = at(nm, [128, 1], F32, o)
    o = at("rmax1", [128, 32], F32, o)
    o = at("d1g", [128, 32], F32, o)
    o = at("d1u", [128, 32], F32, o)
    o = at("rmax2", [128, 32], F32, o)
    o = at("accw", [128, 4], F32, o)
    o = at("vals", [1, 8], F32, o)
    o = at("rsum", [1, 512], F32, o)
    at("stat16", [128, 128], F32, o - 2048)
    o = at("recipc", [128, 8], F32, o)
    o = at("epsb", [128, 1], F32, o)
    o = at("hstat", [128, 52], F32, o)
    assert o <= 6144, o
    o = 6144
    o = at("WdqT", [128, 2048], BF16, o)
    o = at("WdkvT", [128, 2048], BF16, o)
    o = at("A_sb", [128, 8192], BF16, o)
    o = at("V_sb", [128, 2048], BF16, o)
    o = at("q1_fm", [128, 4096], BF16, o)
    o = at("ckv_lm", [128, 4096], BF16, o)
    o = at("ckv_tm", [128, 4096], BF16, o)
    o = at("qlat_sb", [128, 1024], BF16, o)
    o = at("PT0", [128, 512], BF16, o)
    o = at("PT1", [128, 512], BF16, o)
    o = at("ctx_sb", [128, 1024], BF16, o)
    assert o <= 65536, o
    o = 65536
    o = at("ln_scr", [128, 1024], F32, o)
    at("sq_scr", [128, 1024], F32, 65536)
    o = at("tm_bf", [128, 1024], BF16, o)
    lnfm_off = o
    o = at("ln_fm_blk", [128, 4096], BF16, o)
    at("qlat_sb2", [128, 1024], BF16, lnfm_off)
    at("PT2", [128, 512], BF16, lnfm_off + 2048)
    assert o <= 79872, o
    XP0 = 79872
    at("xp", [128, 16384], F32, XP0)
    CD0 = 145408
    at("codes1", [128, 32768], BF16, CD0)
    o = XP0
    o = at("WoT", [128, 8192], BF16, o)
    o = at("Wo_nat", [128, 8192], BF16, o)
    o = at("Wdq_nat", [128, 2048], BF16, o)
    o = at("Wuq_nat", [128, 2048], BF16, o)
    o = at("Wdkv_nat", [128, 2048], BF16, o)
    o = at("Wuk_nat", [128, 2048], BF16, o)
    o = at("Wuv_nat", [128, 2048], BF16, o)
    o = at("T1_sb", [128, 512], BF16, o)
    o = at("M_sb", [128, 2048], BF16, o)
    assert o <= 145408, o
    o = CD0 + 32768
    o = at("wst0", [128, 4096], F32, o)
    o = at("wst1", [128, 4096], F32, o)
    assert o <= 210944, o
    o = CD0
    o = at("codes2_tm1", [128, 4096], BF16, o)
    o = at("codes2_fm1", [128, 4096], BF16, o)
    o = at("wdq_st0", [128, 4096], BF16, o)
    o = at("wdq_st1", [128, 4096], BF16, o)
    assert o <= 210944, o
    o = 6144
    o = at("f_wst0", [128, 4096], F32, o)
    o = at("f_wst1", [128, 4096], F32, o)
    o = at("f_qscr", [128, 4096], F32, o)
    o = at("f_wq", [128, 4096], BF16, o)
    o = at("mgT0", [128, 4096], BF16, o)
    o = at("mgT1", [128, 4096], BF16, o)
    o = at("muT0", [128, 4096], BF16, o)
    o = at("muT1", [128, 4096], BF16, o)
    o = at("gu0", [128, 512], F32, o)
    o = at("gu1", [128, 512], F32, o)
    o = at("sg_sb", [128, 512], F32, o)
    o = at("t1_sb", [128, 512], F32, o)
    o = at("rmax8", [128, 256], F32, o)
    o = at("p4wd_st", [128, 4096], F32, o)
    o = at("p4wd_dq", [128, 4096], BF16, o)
    assert o <= 145408, o
    at("mdownT", [128, 32768], BF16, XP0)
    o = 6144
    o = at("gst0", [128, 4096], F32, o)
    o = at("gst1", [128, 4096], F32, o)
    o = at("codes2_tm", [128, 4096], BF16, o)
    o = at("codes2_fm", [128, 4096], BF16, o)
    o = at("dq_wq", [128, 4096], BF16, o)
    o = at("outst0", [128, 1024], F32, o)
    o = at("outst1", [128, 1024], F32, o)
    o = at("xpst0", [128, 1024], F32, o)
    o = at("xpst1", [128, 1024], F32, o)
    assert o <= 79872, o
    return b


def build_nc(taps=()):
    nc = bass.Bass("TRN2", target_bir_lowering=False,
                   detect_race_conditions=False)
    c = Ctx()
    c.nc = nc
    c.X = nc.dram_tensor("X", [TTOT, C], F32, kind="ExternalInput").ap()
    c.Wdq = nc.dram_tensor("Wdq", [QL, C], F32, kind="ExternalInput").ap()
    c.Wuq = nc.dram_tensor("Wuq", [C, QL], F32, kind="ExternalInput").ap()
    c.Wdkv = nc.dram_tensor("Wdkv", [KVL, C], F32, kind="ExternalInput").ap()
    c.Wuk = nc.dram_tensor("Wuk", [C, KVL], F32, kind="ExternalInput").ap()
    c.Wuv = nc.dram_tensor("Wuv", [C, KVL], F32, kind="ExternalInput").ap()
    c.Wo = nc.dram_tensor("Wo", [C, C], F32, kind="ExternalInput").ap()
    c.WG = nc.dram_tensor("WG", [INNER, C], F32, kind="ExternalInput").ap()
    c.WU = nc.dram_tensor("WU", [INNER, C], F32, kind="ExternalInput").ap()
    c.WD = nc.dram_tensor("WD", [C, INNER], F32, kind="ExternalInput").ap()
    c.OUT = nc.dram_tensor("OUT", [TTOT, C], F32, kind="ExternalOutput").ap()
    c.GU = nc.dram_tensor("GU", [TTOT, INNER], F32, kind="Internal").ap()
    c.XPD = nc.dram_tensor("XPD", [TTOT, C], F32, kind="Internal").ap()
    c.WDQ = nc.dram_tensor("WDQ", [C, INNER], BF16, kind="Internal").ap()

    ident = np.eye(128, dtype=ml_dtypes.bfloat16)
    c.IDENT = nc.inline_tensor(ident, "IDENT").ap()
    p_i = np.arange(128)[:, None]
    j_i = np.arange(896)[None, :]
    maskw = (p_i <= j_i - 384).astype(ml_dtypes.bfloat16)
    c.MASKW = nc.inline_tensor(maskw, "MASKW").ap()

    c.b = _alloc_sbuf(nc)
    c.pA = nc.alloc_psum_tensor("pA", [128, 1024], F32)
    c.pB = nc.alloc_psum_tensor("pB", [128, 1024], F32)
    c.pC = nc.alloc_psum_tensor("pC", [128, 1024], F32)
    c.pD = nc.alloc_psum_tensor("pD", [128, 1024], F32)

    c.taps = {}
    tap_specs = {"A_sb": ([128, 8192], BF16), "V_sb": ([128, 2048], BF16),
                 "q1_fm": ([128, 4096], BF16), "ckv_lm": ([128, 4096], BF16),
                 "ckv_tm": ([128, 4096], BF16), "consts": ([128, 8], F32),
                 "M_sb": ([128, 2048], BF16), "WdqT": ([128, 2048], BF16),
                 "xp0": ([128, 16384], F32), "codes1": ([128, 32768], BF16),
                 "d1g": ([128, 32], F32), "rmax2": ([128, 32], F32),
                 "accw": ([128, 4], F32)}
    for t_ in taps:
        shp, dt_ = tap_specs[t_]
        c.taps[t_] = nc.dram_tensor("tap_" + t_, shp, dt_,
                                    kind="ExternalOutput").ap()

    import contextlib
    stack = contextlib.ExitStack()
    sem_handles = [(nm, stack.enter_context(nc.semaphore(nm)))
                   for nm in ("sSY", "sSC", "sGP", "sPE", "sDV", "sAC", "sPL")]

    with nc.Block() as blk:
        def run_pass(engine_name):
            def body(real):
                for nm, h in sem_handles:
                    setattr(c, nm, S(h))
                c.sy = E(real if engine_name == "sync" else None)
                c.pe = E(real if engine_name == "tensor" else None)
                c.ve = E(real if engine_name == "vector" else None)
                c.ac = E(real if engine_name == "scalar" else None)
                c.gp = E(real if engine_name == "gpsimd" else None)
                program(c)
            return body
        blk.sync(run_pass("sync"))
        blk.tensor(run_pass("tensor"))
        blk.vector(run_pass("vector"))
        blk.scalar(run_pass("scalar"))
        blk.gpsimd(run_pass("gpsimd"))
    stack.close()
    return nc



def program(c):
    p0_consts(c)
    p0_weights(c)
    for s in range(NSEQ):
        p1_proj(c, s)
        p2_attn(c, s, hook=(absmean_hook(c) if s == 0 else None))
        if s == 0 and "xp0" in c.taps:
            c.sy.wait(c.sDV)
            c.sy.op("dma_start", out=c.taps["xp0"], in_=c.b["xp"][:, :],
                    inc16=c.sSY)
        if s == NSEQ - 1:
            p4_preload(c)
        p3_codes1(c, s)
    p4_gateup(c)
    p5_down(c)
    c.sy.wait(c.sSC)
    c.sy.wait(c.sSY)
    c.sy.wait(c.sGP)


def p0_consts(c):
    b, ve, sy = c.b, c.ve, c.sy
    sy.op("dma_start", out=b["ident"][:, :], in_=c.IDENT[:, :], inc16=c.sSY)
    sy.op("dma_start", out=b["maskW"][:, :], in_=c.MASKW[:, :], inc16=c.sSY)
    ve.op("memset", b["ones_bf"][:, :], 1.0)
    ve.op("memset", b["onesr_f"][:, :], 1.0)
    ve.op("memset", b["ones128_f"][:, :], 1.0)
    ve.op("memset", b["epsb"][:, :], 1e-5)
    ve.op("memset", b["dD"][:, :], 1.0, inc=c.sDV)
    ve.wait(c.sDV)
    ve.op("tensor_scalar", b["dD"][:, :], b["dD"][:, :], M64, -M64,
          ALU.add, ALU.add, inc=c.sDV)
    ve.wait(c.sDV)
    ve.op("tensor_scalar", b["mqt"][:, :], b["dD"][:, :], M64 - M32, M32,
          ALU.mult, ALU.add, inc=c.sDV)
    ve.wait(c.sDV)
    ve.op("tensor_scalar_mul", b["nmqt"][:, :], b["mqt"][:, :], -1.0, inc=c.sDV)
    ve.wait(c.sDV)


def _transpose_group(c, srcs, psum, dst):
    pe, ve = c.pe, c.ve
    n = len(srcs)
    for k in range(n):
        pe.op("matmul", psum[:, k * 128:(k + 1) * 128], srcs[k],
              c.b["ident"][:, :], start=True, stop=True,
              inc=(c.sPE if k == n - 1 else None))
    ve.wait(c.sPE)
    ve.op("tensor_copy", dst, psum[:, 0:n * 128], inc=c.sDV)
    return c.sDV.n


def p0_weights(c):
    b, pe, ve, gp = c.b, c.pe, c.ve, c.gp
    ld_snap = {}
    for nm, src in [("Wdq_nat", c.Wdq), ("Wdkv_nat", c.Wdkv),
                    ("Wo_nat", c.Wo), ("Wuq_nat", c.Wuq),
                    ("Wuk_nat", c.Wuk), ("Wuv_nat", c.Wuv)]:
        a = src.shape[0] // 128
        gp.op("dma_start",
              out=b[nm].ap().rearrange("p (a cc) -> p a cc", a=a),
              in_=src.rearrange("(a p) cc -> p a cc", p=128), inc16=c.sGP)
        ld_snap[nm] = c.sGP.n
    pe.wait(c.sSY, 32)

    groups = []
    for cc in range(8):
        groups.append(([b["Wdq_nat"][:, a * 1024 + cc * 128: a * 1024 + (cc + 1) * 128]
                        for a in range(2)], b["WdqT"][:, cc * 256:(cc + 1) * 256]))
    for cc in range(8):
        groups.append(([b["Wdkv_nat"][:, a * 1024 + cc * 128: a * 1024 + (cc + 1) * 128]
                        for a in range(2)], b["WdkvT"][:, cc * 256:(cc + 1) * 256]))
    for cc in range(8):
        groups.append(([b["Wo_nat"][:, oc * 1024 + cc * 128: oc * 1024 + (cc + 1) * 128]
                        for oc in range(8)], b["WoT"][:, cc * 1024:(cc + 1) * 1024]))
    ps = [c.pA, c.pB]
    evac_of = {-2: c.sDV.n, -1: c.sDV.n}
    for gi, (srcs, dst) in enumerate(groups):
        if gi == 0:
            pe.wait(c.sGP, ld_snap["Wdq_nat"])
        elif gi == 8:
            pe.wait(c.sGP, ld_snap["Wdkv_nat"])
        elif gi == 16:
            pe.wait(c.sGP, ld_snap["Wo_nat"])
        psum = ps[gi % 2]
        pe.wait(c.sDV, evac_of[gi - 2])
        n = len(srcs)
        for k in range(n):
            pe.op("matmul", psum[:, k * 128:(k + 1) * 128], srcs[k],
                  b["ident"][:, :], start=True, stop=True,
                  inc=(c.sPE if k == n - 1 else None))
        ve.wait(c.sPE)
        ve.op("tensor_copy", dst, psum[:, 0:n * 128], inc=c.sDV)
        evac_of[gi] = c.sDV.n

    cp = [c.pC, c.pD]
    cev = {-2: c.sDV.n, -1: c.sDV.n}
    ci = 0

    def compute_group(mms, evacs):
        nonlocal ci
        psum = cp[ci % 2]
        pe.wait(c.sDV, cev[ci - 2])
        nmm = len(mms)
        for k, (osl, lhsT, rhs, st_, sp_) in enumerate(mms):
            pe.op("matmul", osl(psum), lhsT, rhs, start=st_, stop=sp_,
                  inc=(c.sPE if k == nmm - 1 else None))
        ve.wait(c.sPE)
        nev = len(evacs)
        for k, (dst, ssl) in enumerate(evacs):
            ve.op("tensor_copy", dst, ssl(psum),
                  inc=(c.sDV if k == nev - 1 else None))
        cev[ci] = c.sDV.n
        ci += 1

    pe.wait(c.sGP, ld_snap["Wuk_nat"])
    for qc in range(2):
        mms = [(lambda p: p[:, 0:256],
                b["Wuq_nat"][:, k * 256 + qc * 128: k * 256 + qc * 128 + 128],
                b["Wuk_nat"][:, k * 256:(k + 1) * 256], k == 0, k == 7)
               for k in range(8)]
        compute_group(mms, [(b["T1_sb"][:, qc * 256:(qc + 1) * 256],
                             lambda p: p[:, 0:256])])
    pe.wait(c.sDV, cev[1])
    for cc in range(8):
        mms = [(lambda p: p[:, 0:256],
                b["Wdq_nat"][:, a * 1024 + cc * 128: a * 1024 + (cc + 1) * 128],
                b["T1_sb"][:, a * 256:(a + 1) * 256], a == 0, a == 1)
               for a in range(2)]
        compute_group(mms, [(b["M_sb"][:, cc * 256:(cc + 1) * 256],
                             lambda p: p[:, 0:256])])
    pe.wait(c.sDV, cev[9])
    for hp in range(NH // 2):
        mms = []
        for sub in range(2):
            h = 2 * hp + sub
            tile, off = h // 2, (h % 2) * 64
            for qc in range(2):
                mms.append(
                    (lambda p, qc=qc, sub=sub:
                     p[:, sub * 512 + qc * 256: sub * 512 + (qc + 1) * 256],
                     b["Wuq_nat"][off:off + 64,
                                  tile * 256 + qc * 128: tile * 256 + qc * 128 + 128],
                     b["M_sb"][off:off + 64, tile * 256:(tile + 1) * 256],
                     True, True))
        compute_group(mms, [(b["A_sb"][:, hp * 1024:(hp + 1) * 1024],
                             lambda p: p[:, 0:1024])])
    pe.wait(c.sDV, evac_of[23])
    pe.wait(c.sGP, ld_snap["Wuv_nat"])
    for lc in range(2):
        mms = []
        for ch in range(2):
            for k in range(8):
                mms.append((lambda p, ch=ch: p[:, ch * 512:(ch + 1) * 512],
                            b["Wuv_nat"][:, k * 256 + lc * 128: k * 256 + lc * 128 + 128],
                            b["WoT"][:, k * 1024 + ch * 512: k * 1024 + (ch + 1) * 512],
                            k == 0, k == 7))
        compute_group(mms, [(b["V_sb"][:, lc * 1024:(lc + 1) * 1024],
                             lambda p: p[:, 0:1024])])
    ve.wait(c.sDV)
    c.p0_pe_done = c.sPE.n
    for t_ in ("M_sb", "A_sb", "V_sb", "WdqT"):
        if t_ in c.taps:
            c.sy.wait(c.sDV)
            c.sy.op("dma_start", out=c.taps[t_], in_=b[t_][:, :], inc16=c.sSY)


def absmean_hook(c):
    b, ve, pe, gp = c.b, c.ve, c.pe, c.gp
    ve.op("memset", b["accw"][:, :], 0.0)
    ve.op("memset", b["vals"][:, :], 0.0, inc=c.sDV)
    ve.wait(c.sDV)
    mats = [(c.WG, 0), (c.WU, 1), (c.WD, 2)]
    chunks = [(src, col, k) for src, col in mats for k in range(8)]
    dve_done = []

    def emit_chunk(i):
        src, col, k = chunks[i]
        st = b["wst0"] if i % 2 == 0 else b["wst1"]
        if i >= 2:
            gp.wait(c.sDV, dve_done[i - 2])
        a_per = 4096 // src.shape[1]
        gp.op("dma_start",
              out=st.ap().rearrange("p (a cc) -> p a cc", a=a_per),
              in_=src.rearrange("(a p) cc -> p a cc", p=128)[
                  :, k * a_per:(k + 1) * a_per, :], inc16=c.sGP)
        ve.wait(c.sGP)
        ve.op("tensor_reduce", b["st_h"][:, :], st[:, :], AX.X, ALU.add,
              apply_absolute_value=True, inc=c.sDV)
        ve.wait(c.sDV)
        ve.op("tensor_tensor", b["accw"][:, col:col + 1], b["st_h"][:, :],
              b["accw"][:, col:col + 1], ALU.add, inc=c.sDV)
        dve_done.append(c.sDV.n)

    def emit_final():
        pe.wait(c.sDV)
        for col in range(3):
            pe.op("matmul", c.pA[0:1, col:col + 1], b["accw"][:, col:col + 1],
                  b["ones128_f"][:, :], start=True, stop=True,
                  inc=(c.sPE if col == 2 else None))
        ve.wait(c.sPE)
        n_elems = float(INNER * C)
        for col in range(3):
            ve.op("tensor_scalar", b["vals"][0:1, col + 3:col + 4],
                  c.pA[0:1, col:col + 1], 1.0 / n_elems, 1e-5, ALU.mult,
                  ALU.max, inc=(c.sDV if col == 2 else None))
        ve.wait(c.sDV)
        for col in range(3):
            ve.op("reciprocal", b["vals"][0:1, col:col + 1],
                  b["vals"][0:1, col + 3:col + 4])
        for col in range(3):
            ve.op("tensor_scalar_mul", b["vals"][0:1, col + 3:col + 4],
                  b["vals"][0:1, col + 3:col + 4], 1.0 / 127.0,
                  inc=(c.sDV if col == 2 else None))
        pe.wait(c.sDV)
        pe.op("matmul", c.pB[:, 0:8], b["onesr_f"][:, :], b["vals"][0:1, :],
              start=True, stop=True, inc=c.sPE)
        ve.wait(c.sPE)
        ve.op("tensor_copy", b["consts"][:, :], c.pB[:, 0:8], inc=c.sDV)
        ve.wait(c.sDV)
        pe.wait(c.sDV)
        if "consts" in c.taps:
            c.sy.wait(c.sDV)
            c.sy.op("dma_start", out=c.taps["consts"], in_=b["consts"][:, :],
                    inc16=c.sSY)
        if "accw" in c.taps:
            c.sy.wait(c.sDV)
            c.sy.op("dma_start", out=c.taps["accw"], in_=b["accw"][:, :],
                    inc16=c.sSY)

    def hook(step):
        if step < 24:
            emit_chunk(step)
        elif step == 24:
            emit_final()
    return hook


def _ln_stats2(c, xin, dump):
    b, ve, ac = c.b, c.ve, c.ac
    ve.op("tensor_reduce", b["st_a"][:, :], xin, AX.X, ALU.add, inc=c.sDV)
    ac.wait(c.sDV)
    ac.op("activation", dump, xin, AF.Square, accum_out=b["st_c"][:, 0:1],
          inc=c.sAC)
    ve.wait(c.sAC)
    ve.wait(c.sDV)
    ve.op("tensor_scalar_mul", b["st_b"][:, :], b["st_a"][:, :], 1.0 / 1024.0)
    ve.op("scalar_tensor_tensor", b["st_d"][:, :], b["st_a"][:, :],
          1.0 / (1024.0 * 1024.0), b["st_a"][:, :], ALU.mult, ALU.mult,
          inc=c.sDV)
    ve.wait(c.sDV)
    ve.op("scalar_tensor_tensor", b["st_d"][:, :], b["st_c"][:, :],
          1.0 / 1024.0, b["st_d"][:, :], ALU.mult, ALU.subtract, inc=c.sDV)
    ac.wait(c.sDV)
    ac.op("activation", b["st_d"][:, :], b["st_d"][:, :], AF.Sqrt,
          bias=b["epsb"][:, 0:1], inc=c.sAC)
    ve.wait(c.sAC)
    ve.op("reciprocal", b["st_e"][:, :], b["st_d"][:, :], inc=c.sDV)
    ve.wait(c.sDV)


def p1_proj(c, s):
    b, pe, ve, ac, sy = c.b, c.pe, c.ve, c.ac, c.sy
    if s > 0:
        half_snap = [c.xp1_ld[7], c.xp1_ld[15]]
    else:
        sy.wait(c.sPE, c.p0_pe_done)
        half_snap = []
        for hf in range(2):
            sy.op("dma_start",
                  out=b["xp"].ap().rearrange("p (i cc) -> p i cc", i=NT)[
                      :, hf * 8:(hf + 1) * 8, :],
                  in_=c.X[s * T + hf * 1024: s * T + (hf + 1) * 1024, :]
                      .rearrange("(i p) cc -> p i cc", p=128),
                  inc16=c.sSY)
            half_snap.append(c.sSY.n)

    SUMq = b["hstat"][:, 0:4]
    SQSq = b["hstat"][:, 4:8]
    MEAN16 = b["hstat"][:, 20:36]
    RSTD16 = b["hstat"][:, 36:52]

    def stats_add(q, i):
        if i == 0:
            ve.wait(c.sSY, half_snap[q // 2])
            ac.wait(c.sSY, half_snap[q // 2])
            ac.wait(c.sDV)
        gi = 4 * q + i
        xpt = b["xp"][:, gi * 1024:(gi + 1) * 1024]
        ve.op("tensor_reduce", b["hstat"][:, i:i + 1], xpt, AX.X, ALU.add,
              inc=(c.sDV if i == 3 else None))
        ac.op("activation", b["sq_scr"][:, :], xpt, AF.Square,
              accum_out=b["hstat"][:, 4 + i:5 + i],
              inc=(c.sAC if i == 3 else None))

    def stats_fin(q):
        ve.wait(c.sAC)
        ve.wait(c.sDV)
        ve.op("tensor_scalar_mul", MEAN16[:, 4 * q:4 * q + 4], SUMq, 1.0 / 1024.0)
        ve.op("scalar_tensor_tensor", SUMq, SUMq, 1.0 / (1024.0 * 1024.0), SUMq,
              ALU.mult, ALU.mult, inc=c.sDV)
        ve.wait(c.sDV)
        ve.op("scalar_tensor_tensor", SQSq, SQSq, 1.0 / 1024.0, SUMq,
              ALU.mult, ALU.subtract, inc=c.sDV)
        ac.wait(c.sDV)
        ac.op("activation", SQSq, SQSq, AF.Sqrt, bias=b["epsb"][:, 0:1],
              inc=c.sAC)
        ve.wait(c.sAC)
        ve.op("reciprocal", RSTD16[:, 4 * q:4 * q + 4], SQSq, inc=c.sDV)
        ve.wait(c.sDV)

    tmb = [b["tm_bf"], b["ctx_sb"]]
    transp_done = [c.sPE.n, c.sPE.n]
    apply_snap = {}
    evac_snap = {-2: c.sDV.n, -1: c.sDV.n}
    q1ckv_done = {-1: c.sPE.n}

    def apply_t(gi):
        xpt = b["xp"][:, gi * 1024:(gi + 1) * 1024]
        ve.wait(c.sPE, transp_done[gi % 2])
        ve.op("tensor_scalar", tmb[gi % 2][:, :], xpt,
              b["hstat"][:, 20 + gi:21 + gi], b["hstat"][:, 36 + gi:37 + gi],
              ALU.subtract, ALU.mult, inc=c.sDV)
        apply_snap[gi] = c.sDV.n

    tp_banks = [c.pA, c.pD]

    def transp(gi):
        tb, i = gi // 4, gi % 4
        psum = tp_banks[gi % 2]
        pe.wait(c.sDV, max(apply_snap[gi], evac_snap[gi - 2]))
        if i == 3:
            pe.wait(c.sDV, evac_snap[gi - 1])
        for k in range(8):
            pe.op("matmul", psum[:, k * 128:(k + 1) * 128],
                  tmb[gi % 2][:, k * 128:(k + 1) * 128], b["ident"][:, :],
                  start=True, stop=True, inc=(c.sPE if k == 7 else None))
        transp_done[gi % 2] = c.sPE.n
        ve.wait(c.sPE, transp_done[gi % 2])
        if i == 0:
            ve.wait(c.sPE, q1ckv_done[tb - 1])
        dst = b["ln_fm_blk"].ap().rearrange("p (cc t) -> p cc t", cc=8)[
            :, :, i * 128:(i + 1) * 128]
        src3 = psum.ap().rearrange("p (cc t) -> p cc t", cc=8)
        ve.op("tensor_copy", dst, src3, inc=c.sDV)
        evac_snap[gi] = c.sDV.n

    def qkv(tb):
        pe.wait(c.sDV, evac_snap[4 * tb + 3])
        for qc in range(2):
            for k in range(8):
                pe.op("matmul", c.pB[:, qc * 512:(qc + 1) * 512],
                      b["WdqT"][:, k * 256 + qc * 128: k * 256 + qc * 128 + 128],
                      b["ln_fm_blk"][:, k * 512:(k + 1) * 512],
                      start=(k == 0), stop=(k == 7),
                      inc=(c.sPE if (qc == 1 and k == 7) else None))
        ve.wait(c.sPE)
        ve.op("tensor_copy", b["q1_fm"][:, tb * 512:(tb + 1) * 512],
              c.pB[:, 0:512])
        ve.op("tensor_copy", b["q1_fm"][:, 2048 + tb * 512: 2048 + (tb + 1) * 512],
              c.pB[:, 512:1024], inc=c.sDV)
        for lc in range(2):
            for k in range(8):
                pe.op("matmul", c.pC[:, lc * 512:(lc + 1) * 512],
                      b["WdkvT"][:, k * 256 + lc * 128: k * 256 + lc * 128 + 128],
                      b["ln_fm_blk"][:, k * 512:(k + 1) * 512],
                      start=(k == 0), stop=(k == 7),
                      inc=(c.sPE if (lc == 1 and k == 7) else None))
        q1ckv_done[tb] = c.sPE.n
        ve.wait(c.sPE)
        ve.op("tensor_copy", b["ckv_lm"][:, tb * 512:(tb + 1) * 512],
              c.pC[:, 0:512])
        ve.op("tensor_copy", b["ckv_lm"][:, 2048 + tb * 512: 2048 + (tb + 1) * 512],
              c.pC[:, 512:1024], inc=c.sDV)
        ckvlm_evac = c.sDV.n
        pe.wait(c.sDV, ckvlm_evac)
        srcs = []
        for i in range(4):
            it = tb * 4 + i
            for lc in range(2):
                srcs.append(b["ckv_lm"][:, lc * 2048 + it * 128:
                                        lc * 2048 + (it + 1) * 128])
        ev = _transpose_group(c, srcs, c.pD,
                              b["ckv_tm"][:, tb * 1024:(tb + 1) * 1024])
        pe.wait(c.sDV, ev)

    for i in range(4):
        stats_add(0, i)
    stats_fin(0)
    for gi in range(NT):
        apply_t(gi)
        if gi >= 1:
            transp(gi - 1)
        if gi >= 4 and gi % 4 == 0:
            qkv(gi // 4 - 1)
        q_next = gi // 4 + 1
        if q_next < NTB:
            stats_add(q_next, gi % 4)
            if gi % 4 == 3:
                stats_fin(q_next)
    transp(NT - 1)
    qkv(NTB - 1)
    if s == 0:
        for t_ in ("q1_fm", "ckv_lm", "ckv_tm"):
            if t_ in c.taps:
                c.sy.wait(c.sDV)
                c.sy.op("dma_start", out=c.taps[t_], in_=b[t_][:, :],
                        inc16=c.sSY)


def p2_attn(c, s, hook=None):
    b, pe, ve, ac = c.b, c.pe, c.ve, c.ac
    sc_banks = [c.pA[:, 0:512], c.pA[:, 512:1024], c.pB[:, 0:512]]
    cx = [c.pB[:, 512:1024], c.pC[:, 0:512]]
    sm = c.pC[:, 512:1024]
    ql = [c.pD[:, 0:512], c.pD[:, 512:1024]]
    yb = c.pD[:, 0:256]
    scols = c.pD[:, 512:516]

    exp_of_bank = [c.sAC.n] * 3
    pt_consumed = [c.sPE.n] * 3
    proj_done = c.sDV.n
    qsb = [b["qlat_sb"], b["qlat_sb2"]]

    st = {
        "scores_done": [c.sPE.n, c.sPE.n],
        "y_done": c.sPE.n,
        "yevac": c.sDV.n,
        "qlat_evac": {},
        "ctx_evac": c.sDV.n,
        "rsum_copy": c.sDV.n,
    }
    heads = [(Tb, h) for Tb in range(NTB) for h in range(NH)]
    pending = None
    p3q_pending = []
    if s == 0:
        c.c1q_done = {}
        c.xp1_ld = {}

    def emit_qlat(hh_):
        if hh_ >= len(heads):
            return
        Tb, h = heads[hh_]
        pe.wait(c.sDV, max(proj_done, st["yevac"]))
        for lc in range(2):
            for qc in range(2):
                pe.op("matmul", ql[lc],
                      b["A_sb"][:, h * 512 + qc * 256 + lc * 128:
                                h * 512 + qc * 256 + (lc + 1) * 128],
                      b["q1_fm"][:, qc * 2048 + Tb * 512:
                                 qc * 2048 + (Tb + 1) * 512],
                      start=(qc == 0), stop=(qc == 1),
                      inc=(c.sPE if (lc == 1 and qc == 1) else None))
        snap_ql = c.sPE.n
        ve.wait(c.sPE, max(snap_ql, st["scores_done"][hh_ % 2]))
        ve.op("tensor_copy", qsb[hh_ % 2][:, 0:512], ql[0])
        ve.op("tensor_copy", qsb[hh_ % 2][:, 512:1024], ql[1], inc=c.sDV)
        st["qlat_evac"][hh_] = c.sDV.n

    def emit_tail(Tb_, h_, hh_):
        ve.wait(c.sPE, st["y_done"])
        ve.op("tensor_copy", b["ctx_sb"][:, 0:512], cx[0])
        ve.op("tensor_copy", b["ctx_sb"][:, 512:1024], cx[1], inc=c.sDV)
        st["ctx_evac"] = c.sDV.n
        pe.wait(c.sDV, max(st["rsum_copy"], st["qlat_evac"][hh_ + 1]
                           if (hh_ + 1) in st["qlat_evac"] else st["rsum_copy"]))
        for k in range(4):
            pe.op("matmul", scols[:, k:k + 1],
                  b["rsum"][0:1, k * 128:(k + 1) * 128],
                  b["onesr_f"][0:1, 0:1], start=True, stop=True,
                  inc=(c.sPE if k == 3 else None))
        snap_scols = c.sPE.n
        ve.wait(c.sPE, snap_scols)
        ve.op("reciprocal", b["recipc"][:, 0:4], scols, inc=c.sDV)
        ve.wait(c.sDV)
        pe.wait(c.sDV, st["ctx_evac"])
        for k in range(4):
            for lc in range(2):
                pe.op("matmul", yb[:, k * 64:(k + 1) * 64],
                      b["ctx_sb"][:, lc * 512 + k * 128:
                                  lc * 512 + (k + 1) * 128],
                      b["V_sb"][:, lc * 1024 + h_ * 64:
                                lc * 1024 + (h_ + 1) * 64],
                      start=(lc == 0), stop=(lc == 1),
                      inc=(c.sPE if (k == 3 and lc == 1) else None))
        st["y_done"] = c.sPE.n
        ve.wait(c.sPE, st["y_done"])
        for k in range(4):
            xpv = b["xp"][:, (Tb_ * 4 + k) * 1024 + h_ * 64:
                          (Tb_ * 4 + k) * 1024 + (h_ + 1) * 64]
            ve.op("scalar_tensor_tensor", xpv, yb[:, k * 64:(k + 1) * 64],
                  b["recipc"][:, k:k + 1], xpv, ALU.mult, ALU.add,
                  inc=(c.sDV if k == 3 else None))
        st["yevac"] = c.sDV.n
        if h_ == NH - 1:
            ac.wait(c.sDV, st["yevac"])
            ac.op("dma_start",
                  out=c.XPD[s * T + Tb_ * 512: s * T + (Tb_ + 1) * 512, :]
                      .rearrange("(i p) cc -> p i cc", p=128),
                  in_=b["xp"].ap().rearrange("p (i cc) -> p i cc", i=NT)[
                      :, Tb_ * 4:(Tb_ + 1) * 4, :], inc16=c.sSC)
            p3q_pending.extend(p3_quarter_slices(c, s, Tb_))

    emit_qlat(0)
    for hh, (Tb, h) in enumerate(heads):
        nj = 4 * Tb + 4
        if pending is not None:
            emit_tail(*pending)
        if p3q_pending:
            p3q_pending.pop(0)()
        if hook is not None:
            hook(hh)
        pending = (Tb, h, hh)
        snap_exp = {}
        snap_mask = {}
        snap_scores = {}

        def emit_scores(j):
            bk = j % 3
            pe.wait(c.sAC, exp_of_bank[bk])
            if j == 0:
                pe.wait(c.sDV, st["qlat_evac"][hh])
            for lc in range(2):
                pe.op("matmul", sc_banks[bk],
                      b["ckv_lm"][:, lc * 2048 + j * 128:
                                  lc * 2048 + (j + 1) * 128],
                      qsb[hh % 2][:, lc * 512:(lc + 1) * 512],
                      start=(lc == 0), stop=(lc == 1),
                      inc=(c.sPE if lc == 1 else None))
            snap_scores[j] = c.sPE.n
            ptb = j % 3
            pt = b["PT%d" % ptb]
            ac.wait(c.sPE, max(snap_scores[j], pt_consumed[ptb]))
            ac.op("activation", pt[:, :], sc_banks[bk], AF.Exp, scale=0.125,
                  inc=c.sAC)
            snap_exp[j] = c.sAC.n
            exp_of_bank[bk] = snap_exp[j]
            if j >= 4 * Tb:
                dlt = (j - 4 * Tb) * 128
                ve.wait(c.sAC, snap_exp[j])
                ve.op("tensor_tensor", pt[:, :], pt[:, :],
                      b["maskW"][:, 384 - dlt:896 - dlt], ALU.mult,
                      inc=c.sDV)
                snap_mask[j] = c.sDV.n

        def emit_ctx(j):
            ptb = j % 3
            pt = b["PT%d" % ptb]
            if j in snap_mask:
                pe.wait(c.sDV, snap_mask[j])
            else:
                pe.wait(c.sAC, snap_exp[j])
            if j == 0:
                pe.wait(c.sDV, max(st["ctx_evac"], st["rsum_copy"]))
            for lc in range(2):
                pe.op("matmul", cx[lc],
                      b["ckv_tm"][:, j * 256 + lc * 128:
                                  j * 256 + (lc + 1) * 128],
                      pt[:, :], start=(j == 0), stop=(j == nj - 1))
            pe.op("matmul", sm, b["ones_bf"][:, 0:128], pt[:, :],
                  start=(j == 0), stop=(j == nj - 1), inc=c.sPE)
            pt_consumed[ptb] = c.sPE.n

        for j in range(nj):
            emit_scores(j)
            if j == 1:
                emit_qlat(hh + 1)
            if j >= 2:
                emit_ctx(j - 2)
        emit_ctx(nj - 2)
        emit_ctx(nj - 1)
        st["scores_done"][hh % 2] = snap_scores[nj - 1]
        snap_ctx_done = c.sPE.n
        ve.wait(c.sPE, snap_ctx_done)
        ve.op("tensor_copy", b["rsum"][0:1, :], sm[0:1, :], inc=c.sDV)
        st["rsum_copy"] = c.sDV.n
    emit_tail(*pending)
    c.attn_tail_pe = c.sPE.n
    c.attn_tail_dv = c.sDV.n
    for f in p3q_pending:
        f()
    p3q_pending.clear()


def p3_quarter_slices(c, s, Tb):
    b, ve, ac = c.b, c.ve, c.ac
    SUMq = b["hstat"][:, 0:4]
    SQSq = b["hstat"][:, 4:8]
    MEANq = b["hstat"][:, 8:12]
    RSTDq = b["hstat"][:, 12:16]
    R1q = b["hstat"][:, 16:20]
    base = 4 * Tb

    def s0():
        ve.wait(c.sDV)
        ac.wait(c.sDV)
        for i in range(4):
            gi = base + i
            xpt = b["xp"][:, gi * 1024:(gi + 1) * 1024]
            ve.op("tensor_reduce", b["hstat"][:, i:i + 1], xpt, AX.X, ALU.add,
                  inc=(c.sDV if i == 3 else None))
            ac.op("activation", b["ln_scr"][:, :], xpt, AF.Square,
                  accum_out=b["hstat"][:, 4 + i:5 + i],
                  inc=(c.sAC if i == 3 else None))

    def s1():
        ve.wait(c.sAC)
        ve.wait(c.sDV)
        ve.op("tensor_scalar_mul", MEANq, SUMq, 1.0 / 1024.0)
        ve.op("scalar_tensor_tensor", SUMq, SUMq, 1.0 / (1024.0 * 1024.0),
              SUMq, ALU.mult, ALU.mult, inc=c.sDV)
        ve.wait(c.sDV)
        ve.op("scalar_tensor_tensor", SQSq, SQSq, 1.0 / 1024.0, SUMq,
              ALU.mult, ALU.subtract, inc=c.sDV)
        ac.wait(c.sDV)
        ac.op("activation", SQSq, SQSq, AF.Sqrt, bias=b["epsb"][:, 0:1],
              inc=c.sAC)
        ve.wait(c.sAC)
        ve.op("reciprocal", RSTDq, SQSq, inc=c.sDV)
        ve.wait(c.sDV)

    def s2():
        for i in range(4):
            gi = base + i
            xpt = b["xp"][:, gi * 1024:(gi + 1) * 1024]
            ve.op("tensor_scalar", b["ln_scr"][:, :], xpt,
                  b["hstat"][:, 8 + i:9 + i], b["hstat"][:, 12 + i:13 + i],
                  ALU.subtract, ALU.mult)
            ve.op("tensor_reduce", b["hstat"][:, 16 + i:17 + i],
                  b["ln_scr"][:, :], AX.X, ALU.max, apply_absolute_value=True,
                  inc=(c.sDV if i == 3 else None))
        ve.wait(c.sDV)

    store_snap = c.sSC.n

    def s3():
        ve.op("tensor_scalar_max", R1q, R1q, 1e-5, inc=c.sDV)
        ve.wait(c.sDV)
        tt0 = s * NT + base
        ve.op("tensor_copy", b["rmax1"][:, tt0:tt0 + 4], R1q)
        ve.op("reciprocal", SUMq, R1q, inc=c.sDV)
        ve.wait(c.sDV)
        ve.op("tensor_scalar_mul", SUMq, SUMq, 127.0, inc=c.sDV)
        ve.wait(c.sDV)
        ve.op("tensor_tensor", b["hstat"][:, 20 + base:20 + base + 4], RSTDq,
              SUMq, ALU.mult, inc=c.sDV)
        ve.wait(c.sDV)
        ve.op("scalar_tensor_tensor", b["hstat"][:, 36 + base:36 + base + 4],
              MEANq, -1.0, b["hstat"][:, 20 + base:20 + base + 4],
              ALU.mult, ALU.mult, inc=c.sDV)
        ve.wait(c.sDV)

    def mk_q(i):
        def qz():
            gi = base + i
            tt = s * NT + gi
            xpt = b["xp"][:, gi * 1024:(gi + 1) * 1024]
            ve.op("tensor_scalar", b["ln_scr"][:, :], xpt,
                  b["hstat"][:, 20 + gi:21 + gi],
                  b["hstat"][:, 36 + gi:37 + gi], ALU.mult, ALU.add,
                  inc=c.sDV)
            xp_free = c.sDV.n
            ve.op("tensor_scalar", b["codes1"][:, tt * 1024:(tt + 1) * 1024],
                  b["ln_scr"][:, :], b["mqt"][:, 0:1], b["nmqt"][:, 0:1],
                  ALU.add, ALU.add, inc=c.sDV)
            c.c1q_done[tt] = c.sDV.n
            if s == 0:
                c.sy.wait(c.sDV, xp_free)
                c.sy.wait(c.sSC, store_snap)
                c.sy.op("dma_start",
                        out=b["xp"].ap().rearrange(
                            "p (i cc) -> p i cc", i=NT)[:, gi:gi + 1, :],
                        in_=c.X[T + gi * 128: T + (gi + 1) * 128, :]
                            .rearrange("(i p) cc -> p i cc", p=128),
                        inc16=c.sSY)
                c.xp1_ld[gi] = c.sSY.n
        return qz

    return [s0, s1, s2, s3, mk_q(0), mk_q(1), mk_q(2), mk_q(3)]


def p3_codes1(c, s):
    b, pe, ve, ac = c.b, c.pe, c.ve, c.ac
    ev = {-2: c.sAC.n, -1: c.sAC.n}
    ps = [c.pA, c.pB]
    c.c1t_done = None
    for gi in range(NT):
        tt = s * NT + gi
        psum = ps[gi % 2]
        pe.wait(c.sDV, c.c1q_done[tt])
        pe.wait(c.sAC, ev[gi - 2])
        for k in range(8):
            pe.op("matmul", psum[:, k * 128:(k + 1) * 128],
                  b["codes1"][:, tt * 1024 + k * 128: tt * 1024 + (k + 1) * 128],
                  b["ident"][:, :], start=True, stop=True,
                  inc=(c.sPE if k == 7 else None))
        ac.wait(c.sPE)
        ac.op("activation", b["codes1"][:, tt * 1024:(tt + 1) * 1024],
              psum[:, 0:1024], AF.Copy, inc=c.sAC)
        ev[gi] = c.sAC.n
    c.c1t_done = c.sAC.n
    if s == NSEQ - 1:
        for t_ in ("codes1", "d1g"):
            if t_ in c.taps:
                c.sy.wait(c.sDV)
                c.sy.op("dma_start", out=c.taps[t_], in_=b[t_][:, :],
                        inc16=c.sSY)


def p4_preload(c):
    b = c.b
    c.sy.wait(c.sPE)
    c.sy.wait(c.sDV)
    c.p4_ld = {}
    for m_i, src in enumerate([c.WG, c.WU]):
        c.sy.op("dma_start",
                out=[b["f_wst0"], b["f_wst1"]][m_i].ap().rearrange(
                    "p (a cc) -> p a cc", a=4),
                in_=src[0:512, :].rearrange("(a p) cc -> p a cc", p=128),
                inc16=c.sSY)
        c.p4_ld[(0, m_i)] = c.sSY.n


def p4_gateup(c):
    b, pe, ve, ac = c.b, c.pe, c.ve, c.ac
    ve.wait(c.sDV)
    ve.op("tensor_scalar", b["d1g"][:, :], b["rmax1"][:, :],
          b["consts"][:, 3:4], None, ALU.mult, ALU.bypass)
    ve.op("tensor_scalar", b["d1u"][:, :], b["rmax1"][:, :],
          b["consts"][:, 4:5], None, ALU.mult, ALU.bypass, inc=c.sDV)
    ve.wait(c.sDV)
    codes_ready = c.sDV.n
    mg = [b["mgT0"], b["mgT1"]]
    mu = [b["muT0"], b["muT1"]]
    wst = [b["f_wst0"], b["f_wst1"]]
    gu = [b["gu0"], b["gu1"]]
    gate_b = [c.pA[:, 0:512], c.pB[:, 0:512]]
    up_b = [c.pA[:, 512:1024], c.pB[:, 512:1024]]
    wq_used = {-2: c.sDV.n, -1: c.sDV.n}
    ld = dict(c.p4_ld)
    gu_store_of_buf = [c.sSC.n, c.sSC.n]
    dve_of_bank = [codes_ready, codes_ready]
    wt_consumed = {-2: c.sPE.n, -1: c.sPE.n}
    wt_ready = {}
    sg_consumed = c.sDV.n
    c.sy.wait(c.sSC)
    wdp_q = {-1: c.sDV.n}
    wdp_ld = {}
    c.wdp_store = {}

    def wdp(w, tt):
        if w >= 8:
            return
        st_ = b["p4wd_st"]
        dq_ = b["p4wd_dq"]
        if tt == 0:
            c.sy.wait(c.sDV, wdp_q[w - 1])
            c.sy.op("dma_start", out=st_[:, :],
                    in_=c.WD[w * 128:(w + 1) * 128, :], inc16=c.sSY)
            wdp_ld[w] = c.sSY.n
        elif 2 <= tt <= 9:
            p = tt - 2
            sl = slice(p * 512, (p + 1) * 512)
            if p == 0:
                ve.wait(c.sSY, wdp_ld[w])
            ve.op("tensor_scalar", st_[:, sl], st_[:, sl],
                  b["consts"][:, 2:3], b["mqt"][:, 0:1], ALU.mult, ALU.add)
        elif 10 <= tt <= 17:
            p = tt - 10
            sl = slice(p * 512, (p + 1) * 512)
            ve.op("tensor_scalar", st_[:, sl], st_[:, sl],
                  b["nmqt"][:, 0:1], 1.0, ALU.add, ALU.min)
        elif 18 <= tt <= 25:
            p = tt - 18
            sl = slice(p * 512, (p + 1) * 512)
            if p == 0:
                ve.wait(c.sSC, c.wdp_store.get(w - 1, 0))
            ve.op("tensor_scalar_max", dq_[:, sl], st_[:, sl], -1.0,
                  inc=(c.sDV if p == 7 else None))
            if p == 7:
                wdp_q[w] = c.sDV.n
        elif tt == 27:
            c.sy.wait(c.sDV, wdp_q[w])
            c.sy.op("dma_start", out=c.WDQ[w * 128:(w + 1) * 128, :],
                    in_=dq_[:, :], inc16=c.sSC)
            c.wdp_store[w] = c.sSC.n

    def load_ws(os_):
        if os_ >= 8:
            return
        if os_ < 2:
            c.sy.wait(c.sPE)
            c.sy.wait(c.sDV)
        for m_i, src in enumerate([c.WG, c.WU]):
            c.sy.wait(c.sDV, wq_used[os_ - 1])
            c.sy.op("dma_start",
                    out=wst[m_i].ap().rearrange("p (a cc) -> p a cc", a=4),
                    in_=src[os_ * 512:(os_ + 1) * 512, :].rearrange(
                        "(a p) cc -> p a cc", p=128), inc16=c.sSY)
            ld[(os_, m_i)] = c.sSY.n

    def prep_quant(os_, m_i, part):
        if os_ >= 8:
            return
        sl = slice(part * 512, (part + 1) * 512)
        if part == 0:
            ve.wait(c.sSY, ld[(os_, m_i)])
            ve.wait(c.sPE, wt_consumed[os_ - 2])
            ve.wait(c.sPE)
        ve.op("tensor_scalar", b["f_qscr"][:, sl], wst[m_i][:, sl],
              b["consts"][:, m_i:m_i + 1], b["mqt"][:, 0:1],
              ALU.mult, ALU.add)
        ve.op("tensor_scalar", b["f_qscr"][:, sl], b["f_qscr"][:, sl],
              b["nmqt"][:, 0:1], 1.0, ALU.add, ALU.min)
        ve.op("tensor_scalar_max", b["f_wq"][:, sl], b["f_qscr"][:, sl],
              -1.0, inc=(c.sDV if part == 7 else None))
        if part == 7 and m_i == 1:
            wq_used[os_] = c.sDV.n

    pp_evac = {-2: c.sAC.n, -1: c.sAC.n}
    pp_ps = [c.pC, c.pD]
    pp_ctr = [0]

    def prep_transp(os_, m_i, half):
        if os_ >= 8:
            return
        buf = os_ % 2
        dstT = (mg if m_i == 0 else mu)[buf]
        pe.wait(c.sDV)
        for cc in range(half * 4, half * 4 + 4):
            gk = pp_ctr[0]
            pp_ctr[0] += 1
            psum = pp_ps[gk % 2]
            pe.wait(c.sAC, pp_evac[gk - 2])
            srcs = [b["f_wq"][:, a * 1024 + cc * 128: a * 1024 + (cc + 1) * 128]
                    for a in range(4)]
            for k in range(4):
                pe.op("matmul", psum[:, k * 128:(k + 1) * 128], srcs[k],
                      b["ident"][:, :], start=True, stop=True,
                      inc=(c.sPE if k == 3 else None))
            ac.wait(c.sPE)
            ac.op("activation", dstT[:, cc * 512:(cc + 1) * 512],
                  psum[:, 0:512], AF.Copy, inc=c.sAC)
            pp_evac[gk] = c.sAC.n
        if half == 1:
            wt_ready[(os_, m_i)] = c.sAC.n

    load_ws(0)
    for part in range(8):
        prep_quant(0, 0, part)
    prep_transp(0, 0, 0)
    prep_transp(0, 0, 1)
    for part in range(8):
        prep_quant(0, 1, part)
    prep_transp(0, 1, 0)
    prep_transp(0, 1, 1)
    load_ws(1)
    for os_ in range(8):
        buf = os_ % 2
        pe.wait(c.sAC, wt_ready[(os_, 1)])
        if os_ == 0:
            pe.wait(c.sDV, codes_ready)
            pe.wait(c.sAC, c.c1t_done)
        last_u = c.sPE.n
        for tt in range(NTT):
            pb = tt % 2
            gb, ub = gate_b[pb], up_b[pb]
            pe.wait(c.sDV, dve_of_bank[pb])
            for cc in range(8):
                pe.op("matmul", gb,
                      b["codes1"][:, tt * 1024 + cc * 128:
                                  tt * 1024 + (cc + 1) * 128],
                      mg[buf][:, cc * 512:(cc + 1) * 512],
                      start=(cc == 0), stop=(cc == 7),
                      inc=(c.sPE if cc == 7 else None))
            snap_g = c.sPE.n
            for cc in range(8):
                pe.op("matmul", ub,
                      b["codes1"][:, tt * 1024 + cc * 128:
                                  tt * 1024 + (cc + 1) * 128],
                      mu[buf][:, cc * 512:(cc + 1) * 512],
                      start=(cc == 0), stop=(cc == 7),
                      inc=(c.sPE if cc == 7 else None))
            snap_u = c.sPE.n
            last_u = snap_u
            ac.wait(c.sPE, snap_g)
            ac.wait(c.sDV, sg_consumed)
            ac.op("activation", b["sg_sb"][:, :], gb, AF.Sigmoid,
                  scale=b["d1g"][:, tt:tt + 1], inc=c.sAC)
            ve.wait(c.sAC)
            gub = gu[pb]
            ve.wait(c.sSC, gu_store_of_buf[pb])
            ve.op("scalar_tensor_tensor", b["t1_sb"][:, :], gb,
                  b["d1g"][:, tt:tt + 1], b["sg_sb"][:, :], ALU.mult, ALU.mult)
            sg_consumed = c.sDV.n + 1
            ve.wait(c.sPE, snap_u)
            ve.op("tensor_tensor", gub[:, :], b["t1_sb"][:, :], ub, ALU.mult,
                  inc=c.sDV)
            dve_of_bank[pb] = c.sDV.n
            ve.op("tensor_reduce", b["rmax8"][:, os_ * 32 + tt:os_ * 32 + tt + 1],
                  gub[:, :], AX.X, ALU.max, apply_absolute_value=True)
            ac.wait(c.sDV, dve_of_bank[pb])
            ac.op("dma_start",
                  out=c.GU[tt * 128:(tt + 1) * 128, os_ * 512:(os_ + 1) * 512],
                  in_=gub[:, :], inc16=c.sSC)
            gu_store_of_buf[pb] = c.sSC.n
            if 3 <= tt <= 10:
                prep_quant(os_ + 1, 0, tt - 3)
            elif tt == 11:
                prep_transp(os_ + 1, 0, 0)
            elif tt == 13:
                prep_transp(os_ + 1, 0, 1)
            elif 19 <= tt <= 26:
                prep_quant(os_ + 1, 1, tt - 19)
            elif tt == 27:
                prep_transp(os_ + 1, 1, 0)
            elif tt == 29:
                prep_transp(os_ + 1, 1, 1)
            elif tt == 30:
                load_ws(os_ + 2)
            wdp(os_, tt)
        wt_consumed[os_] = last_u
    ve.op("engine_nop", inc=c.sDV)
    ve.wait(c.sDV)
    if "rmax2" in c.taps:
        c.sy.wait(c.sDV)
        c.sy.op("dma_start", out=c.taps["rmax2"], in_=b["rmax2"][:, :],
                inc16=c.sSY)


def p5_down(c):
    b, pe, ve, ac = c.b, c.pe, c.ve, c.ac
    gst = [b["gst0"], b["gst1"]]
    ve.wait(c.sSC)
    r8 = b["rmax8"]
    ve.op("tensor_tensor", r8[:, 0:64], r8[:, 0:64], r8[:, 64:128], ALU.max,
          inc=c.sDV)
    ve.wait(c.sDV)
    ve.op("tensor_tensor", r8[:, 0:32], r8[:, 0:32], r8[:, 32:64], ALU.max,
          inc=c.sDV)
    ve.wait(c.sDV)
    ve.op("tensor_tensor", b["rmax2"][:, :], r8[:, 0:32],
          r8[:, 128:160], ALU.max, inc=c.sDV)
    ve.wait(c.sDV)
    ve.op("tensor_tensor", b["rmax2"][:, :], b["rmax2"][:, :],
          r8[:, 160:192], ALU.max, inc=c.sDV)
    ve.wait(c.sDV)
    ve.op("tensor_tensor", b["rmax2"][:, :], b["rmax2"][:, :],
          r8[:, 192:224], ALU.max, inc=c.sDV)
    ve.wait(c.sDV)
    ve.op("tensor_tensor", b["rmax2"][:, :], b["rmax2"][:, :],
          r8[:, 224:256], ALU.max, inc=c.sDV)
    ve.wait(c.sDV)
    wdq_st = [b["wdq_st0"], b["wdq_st1"]]
    wdq_ld = {}
    wd_tpe = {}
    g_ctr = [0]
    g_hist = {}

    def wdq_load(w):
        if w >= 8:
            return
        if w < 2:
            c.sy.wait(c.sPE)
        else:
            c.sy.wait(c.sPE, wd_tpe[w - 2])
        c.sy.op("dma_start", out=wdq_st[w % 2][:, :],
                in_=c.WDQ[w * 128:(w + 1) * 128, :], inc16=c.sSY)
        wdq_ld[w] = c.sSY.n

    down_b = [c.pA, c.pB]
    xst = [b["xpst0"], b["xpst1"]]
    ost = [b["outst0"], b["outst1"]]
    c2tm = [b["codes2_tm"], b["codes2_tm1"]]
    c2fm = [b["codes2_fm"], b["codes2_fm1"]]
    gu_ld = {}
    xp_ld = {}
    quant_done_of_buf = [c.sDV.n, c.sDV.n]
    tm_consumed = [c.sPE.n, c.sPE.n]
    fm_consumed = [c.sPE.n, c.sPE.n]
    out_store_of_buf = [c.sSC.n, c.sSC.n]
    down_evac_of_buf = [c.sDV.n, c.sDV.n]
    down_done = {}

    def prefetch_gu(t):
        q = t % 2
        c.sy.wait(c.sDV, quant_done_of_buf[q])
        c.sy.op("dma_start", out=gst[q][:, :],
                in_=c.GU[t * 128:(t + 1) * 128, :], inc16=c.sSY)
        gu_ld[t] = c.sSY.n

    def prefetch_xp(t):
        q = t % 2
        c.sy.wait(c.sDV, down_evac_of_buf[q])
        c.sy.op("dma_start", out=xst[q][:, :],
                in_=c.XPD[t * 128:(t + 1) * 128, :], inc16=c.sSC)
        xp_ld[t] = c.sSC.n

    def emit_quant(tt):
        pb = tt % 2
        ve.wait(c.sSY, gu_ld[tt])
        ve.op("tensor_tensor", b["st_a"][:, :], b["rmax2"][:, tt:tt + 1],
              b["d1u"][:, tt:tt + 1], ALU.mult, inc=c.sDV)
        ve.wait(c.sDV)
        ve.op("tensor_scalar_max", b["st_a"][:, :], b["st_a"][:, :], 1e-5,
              inc=c.sDV)
        ve.wait(c.sDV)
        ve.op("reciprocal", b["st_b"][:, :], b["st_a"][:, :], inc=c.sDV)
        ve.wait(c.sDV)
        q4 = tt % 4
        ve.op("scalar_tensor_tensor", b["recipc"][:, 4 + q4:5 + q4],
              b["st_b"][:, :], 127.0, b["d1u"][:, tt:tt + 1],
              ALU.mult, ALU.mult)
        ve.op("tensor_tensor", b["recipc"][:, q4:q4 + 1], b["st_a"][:, :],
              b["consts"][:, 5:6], ALU.mult, inc=c.sDV)
        ve.wait(c.sDV)
        ve.wait(c.sPE, tm_consumed[pb])
        ve.op("tensor_scalar", gst[pb][:, :], gst[pb][:, :],
              b["recipc"][:, 4 + q4:5 + q4], b["mqt"][:, 0:1],
              ALU.mult, ALU.add)
        ve.op("tensor_scalar", c2tm[pb][:, :], gst[pb][:, :],
              b["nmqt"][:, 0:1], None, ALU.add, ALU.bypass, inc=c.sDV)
        quant_done_of_buf[pb] = c.sDV.n

    tr_evac_g = {}
    tr_ps = [c.pC, c.pD]

    def transp8(srcs, dst, view, eng, sem):
        gi = g_ctr[0]
        g_ctr[0] += 1
        psum = tr_ps[gi % 2]
        if gi - 2 in g_hist:
            sm, cnt = g_hist[gi - 2]
            pe.wait(sm, cnt)
        for k in range(8):
            pe.op("matmul", psum[:, k * 128:(k + 1) * 128], srcs[k],
                  b["ident"][:, :], start=True, stop=True,
                  inc=(c.sPE if k == 7 else None))
        eng.wait(c.sPE)
        if eng is ac:
            eng.op("activation", dst, view(psum), AF.Copy, inc=sem)
        else:
            eng.op("tensor_copy", dst, view(psum), inc=sem)
        g_hist[gi] = (sem, sem.n)
        return (sem, sem.n)

    def wd_transp(w):
        pe.wait(c.sSY, wdq_ld[w])
        for g in range(4):
            srcs = [wdq_st[w % 2][:, (g * 8 + k) * 128:
                                  (g * 8 + k + 1) * 128] for k in range(8)]
            dst = b["mdownT"].ap().rearrange("p (oc t) -> p oc t", oc=32)[
                :, g * 8:(g + 1) * 8, w * 128:(w + 1) * 128]
            transp8(srcs, dst,
                    lambda ps: ps.ap().rearrange("p (k t) -> p k t", k=8),
                    ve, c.sDV)
        wd_tpe[w] = c.sPE.n

    def emit_transposes(tt):
        pb = tt % 2
        pe.wait(c.sDV, quant_done_of_buf[pb])
        pe.wait(c.sPE, fm_consumed[pb])
        for g in range(4):
            srcs = [c2tm[pb][:, (g * 8 + k) * 128:(g * 8 + k + 1) * 128]
                    for k in range(8)]
            tr_evac_g[(tt, g)] = transp8(
                srcs, c2fm[pb][:, g * 1024:(g + 1) * 1024],
                lambda ps: ps[:, 0:1024], ac, c.sAC)
        tm_consumed[pb] = c.sPE.n

    def emit_down(tt):
        pb = tt % 2
        pe.wait(c.sDV, down_evac_of_buf[pb])
        if tt == 0:
            pe.wait(c.sDV, c.mdown_dv)
        for g in range(4):
            sm, cnt = tr_evac_g[(tt, g)]
            pe.wait(sm, cnt)
            for oc in range(g * 8, g * 8 + 8):
                for half in range(2):
                    pe.op("matmul",
                          down_b[pb][:, half * 512:(half + 1) * 512],
                          c2fm[pb][:, oc * 128:(oc + 1) * 128],
                          b["mdownT"][:, oc * 1024 + half * 512:
                                      oc * 1024 + (half + 1) * 512],
                          start=(oc == 0), stop=(oc == 31),
                          inc=(c.sPE if (oc == 31 and half == 1) else None))
        down_done[tt] = c.sPE.n
        fm_consumed[pb] = c.sPE.n
        ve.wait(c.sPE, down_done[tt])
        ve.wait(c.sSC, max(xp_ld[tt], out_store_of_buf[pb]))
        for half in range(2):
            ve.op("scalar_tensor_tensor",
                  ost[pb][:, half * 512:(half + 1) * 512],
                  down_b[pb][:, half * 512:(half + 1) * 512],
                  b["recipc"][:, tt % 4:tt % 4 + 1],
                  xst[pb][:, half * 512:(half + 1) * 512],
                  ALU.mult, ALU.add, inc=(c.sDV if half == 1 else None))
        down_evac_of_buf[pb] = c.sDV.n
        c.sy.wait(c.sDV, down_evac_of_buf[pb])
        c.sy.op("dma_start", out=c.OUT[tt * 128:(tt + 1) * 128, :],
                in_=ost[pb][:, :], inc16=c.sSC)
        out_store_of_buf[pb] = c.sSC.n
        if tt + 1 < NTT:
            prefetch_xp(tt + 1)

    c.sy.wait(c.sSC)
    wdq_load(0)
    prefetch_gu(0)
    prefetch_xp(0)
    wdq_load(1)
    prefetch_gu(1)
    emit_quant(0)
    wd_transp(0)
    wdq_load(2)
    emit_transposes(0)
    emit_quant(1)
    wd_transp(1)
    wdq_load(3)
    emit_transposes(1)
    prefetch_gu(2)
    emit_quant(2)
    for w in range(2, 8):
        wd_transp(w)
        wdq_load(w + 2)
    c.mdown_dv = c.sDV.n
    for tt in range(NTT):
        if 2 < tt + 2 < NTT:
            prefetch_gu(tt + 2)
        if tt >= 1:
            emit_down(tt - 1)
        if tt >= 2:
            emit_transposes(tt)
        if 2 < tt + 1 < NTT:
            emit_quant(tt + 1)
    emit_down(NTT - 1)



def kernel(L0_attn_x, L0_attn_W_dq, L0_attn_W_uq, L0_attn_W_dkv, L0_attn_W_uk,
           L0_attn_W_uv, L0_attn_W_o, L0_ffn_w_gate, L0_ffn_w_up,
           L0_ffn_w_down):
    x = np.ascontiguousarray(np.asarray(L0_attn_x, dtype=np.float32))
    w = {
        "Wdq": np.ascontiguousarray(np.asarray(L0_attn_W_dq, dtype=np.float32)),
        "Wuq": np.ascontiguousarray(np.asarray(L0_attn_W_uq, dtype=np.float32)),
        "Wdkv": np.ascontiguousarray(np.asarray(L0_attn_W_dkv, dtype=np.float32)),
        "Wuk": np.ascontiguousarray(np.asarray(L0_attn_W_uk, dtype=np.float32)),
        "Wuv": np.ascontiguousarray(np.asarray(L0_attn_W_uv, dtype=np.float32)),
        "Wo": np.ascontiguousarray(np.asarray(L0_attn_W_o, dtype=np.float32)),
        "WG": np.ascontiguousarray(np.asarray(L0_ffn_w_gate, dtype=np.float32)),
        "WU": np.ascontiguousarray(np.asarray(L0_ffn_w_up, dtype=np.float32)),
        "WD": np.ascontiguousarray(np.asarray(L0_ffn_w_down, dtype=np.float32)),
    }
    nc = build_nc()
    in_maps = []
    for core in range(8):
        m = {"X": np.ascontiguousarray(x[2 * core:2 * core + 2].reshape(TTOT, C))}
        m.update(w)
        in_maps.append(m)
    res = run_bass_kernel_spmd(nc, in_maps, core_ids=list(range(8)))
    out = np.empty((16, T, C), dtype=np.float32)
    for core in range(8):
        out[2 * core:2 * core + 2] = res.results[core]["OUT"].reshape(2, T, C)
    return out


# version 28
# speedup vs baseline: 1.0170x; 1.0170x over previous
import numpy as np
import ml_dtypes
import concourse.bass as bass
import concourse.mybir as mybir
from concourse.bass_utils import run_bass_kernel_spmd

F32 = mybir.dt.float32
BF16 = mybir.dt.bfloat16
AF = mybir.ActivationFunctionType
ALU = mybir.AluOpType
AX = mybir.AxisListType

M32 = 12582912.0
M64 = 6755399441055744.0

T = 2048
C = 1024
NH = 16
KVL = 256
QL = 256
INNER = 4096
NSEQ = 2
NT = T // 128
NTB = T // 512
TTOT = NSEQ * T
NTT = TTOT // 128


class S:
    __slots__ = ("h", "n")
    def __init__(self, h):
        self.h = h
        self.n = 0


class E:
    def __init__(self, real):
        self.real = real

    def op(self, meth, *a, inc=None, inc16=None, **k):
        ins = None
        if self.real is not None:
            ins = getattr(self.real, meth)(*a, **k)
        if inc is not None:
            inc.n += 1
            if ins is not None:
                ins.then_inc(inc.h, 1)
        if inc16 is not None:
            inc16.n += 16
            if ins is not None:
                ins.then_inc(inc16.h, 16)
        return ins

    def wait(self, sem, val=None):
        v = sem.n if val is None else val
        if self.real is not None:
            self.real.wait_ge(sem.h, v)


class Ctx:
    pass


def _alloc_sbuf(nc):
    b = {}
    BASE = 17408
    def at(name, shape, dtype, off):
        off = (off + 31) & ~31
        b[name] = nc.alloc_sbuf_tensor_at(name, shape, dtype, offset=BASE + off)
        return off + shape[1] * mybir.dt.size(dtype)
    o = 0
    o = at("ident", [128, 128], BF16, o)
    o = at("maskW", [128, 896], BF16, o)
    o = at("ones_bf", [128, 128], BF16, o)
    o = at("onesr_f", [1, 128], F32, o)
    o = at("ones128_f", [128, 1], F32, o)
    o = at("dD", [128, 1], F32, o)
    o = at("mqt", [128, 1], F32, o)
    o = at("nmqt", [128, 1], F32, o)
    o = at("consts", [128, 8], F32, o)
    for nm in ("st_a", "st_b", "st_c", "st_d", "st_e", "st_f", "st_g", "st_h"):
        o = at(nm, [128, 1], F32, o)
    o = at("rmax1", [128, 32], F32, o)
    o = at("d1g", [128, 32], F32, o)
    o = at("d1u", [128, 32], F32, o)
    o = at("rmax2", [128, 32], F32, o)
    o = at("accw", [128, 4], F32, o)
    o = at("vals", [1, 8], F32, o)
    o = at("rsum", [1, 512], F32, o)
    at("stat16", [128, 128], F32, o - 2048)
    o = at("recipc", [128, 8], F32, o)
    o = at("epsb", [128, 1], F32, o)
    o = at("hstat", [128, 52], F32, o)
    assert o <= 6144, o
    o = 6144
    o = at("WdqT", [128, 2048], BF16, o)
    o = at("WdkvT", [128, 2048], BF16, o)
    o = at("A_sb", [128, 8192], BF16, o)
    o = at("V_sb", [128, 2048], BF16, o)
    o = at("q1_fm", [128, 4096], BF16, o)
    o = at("ckv_lm", [128, 4096], BF16, o)
    o = at("ckv_tm", [128, 4096], BF16, o)
    o = at("qlat_sb", [128, 1024], BF16, o)
    o = at("PT0", [128, 512], BF16, o)
    o = at("PT1", [128, 512], BF16, o)
    o = at("ctx_sb", [128, 1024], BF16, o)
    assert o <= 65536, o
    o = 65536
    o = at("ln_scr", [128, 1024], F32, o)
    at("sq_scr", [128, 1024], F32, 65536)
    o = at("tm_bf", [128, 1024], BF16, o)
    lnfm_off = o
    o = at("ln_fm_blk", [128, 4096], BF16, o)
    at("qlat_sb2", [128, 1024], BF16, lnfm_off)
    at("PT2", [128, 512], BF16, lnfm_off + 2048)
    assert o <= 79872, o
    XP0 = 79872
    at("xp", [128, 16384], F32, XP0)
    CD0 = 145408
    at("codes1", [128, 32768], BF16, CD0)
    o = XP0
    o = at("WoT", [128, 8192], BF16, o)
    o = at("Wo_nat", [128, 8192], BF16, o)
    o = at("Wdq_nat", [128, 2048], BF16, o)
    o = at("Wuq_nat", [128, 2048], BF16, o)
    o = at("Wdkv_nat", [128, 2048], BF16, o)
    o = at("Wuk_nat", [128, 2048], BF16, o)
    o = at("Wuv_nat", [128, 2048], BF16, o)
    o = at("T1_sb", [128, 512], BF16, o)
    o = at("M_sb", [128, 2048], BF16, o)
    assert o <= 145408, o
    o = CD0 + 32768
    o = at("wst0", [128, 4096], F32, o)
    o = at("wst1", [128, 4096], F32, o)
    assert o <= 210944, o
    o = CD0
    o = at("codes2_tm1", [128, 4096], BF16, o)
    o = at("codes2_fm1", [128, 4096], BF16, o)
    o = at("wdq_st0", [128, 4096], BF16, o)
    o = at("wdq_st1", [128, 4096], BF16, o)
    assert o <= 210944, o
    o = 6144
    o = at("f_wst0", [128, 4096], F32, o)
    o = at("f_wst1", [128, 4096], F32, o)
    o = at("f_qscr", [128, 4096], F32, o)
    o = at("f_wq", [128, 4096], BF16, o)
    o = at("mgT0", [128, 4096], BF16, o)
    o = at("mgT1", [128, 4096], BF16, o)
    o = at("muT0", [128, 4096], BF16, o)
    o = at("muT1", [128, 4096], BF16, o)
    o = at("gu0", [128, 512], F32, o)
    o = at("gu1", [128, 512], F32, o)
    o = at("sg_sb", [128, 512], F32, o)
    o = at("t1_sb", [128, 512], F32, o)
    o = at("rmax8", [128, 256], F32, o)
    o = at("p4wd_st", [128, 4096], F32, o)
    o = at("p4wd_dq", [128, 4096], BF16, o)
    assert o <= 145408, o
    at("mdownT", [128, 32768], BF16, XP0)
    o = 6144
    o = at("gst0", [128, 4096], F32, o)
    o = at("gst1", [128, 4096], F32, o)
    o = at("codes2_tm", [128, 4096], BF16, o)
    o = at("codes2_fm", [128, 4096], BF16, o)
    o = at("dq_wq", [128, 4096], BF16, o)
    o = at("outst0", [128, 1024], F32, o)
    o = at("outst1", [128, 1024], F32, o)
    o = at("xpst0", [128, 1024], F32, o)
    o = at("xpst1", [128, 1024], F32, o)
    assert o <= 79872, o
    return b


def build_nc(taps=()):
    nc = bass.Bass("TRN2", target_bir_lowering=False,
                   detect_race_conditions=False)
    c = Ctx()
    c.nc = nc
    c.X = nc.dram_tensor("X", [TTOT, C], F32, kind="ExternalInput").ap()
    c.Wdq = nc.dram_tensor("Wdq", [QL, C], F32, kind="ExternalInput").ap()
    c.Wuq = nc.dram_tensor("Wuq", [C, QL], F32, kind="ExternalInput").ap()
    c.Wdkv = nc.dram_tensor("Wdkv", [KVL, C], F32, kind="ExternalInput").ap()
    c.Wuk = nc.dram_tensor("Wuk", [C, KVL], F32, kind="ExternalInput").ap()
    c.Wuv = nc.dram_tensor("Wuv", [C, KVL], F32, kind="ExternalInput").ap()
    c.Wo = nc.dram_tensor("Wo", [C, C], F32, kind="ExternalInput").ap()
    c.WG = nc.dram_tensor("WG", [INNER, C], F32, kind="ExternalInput").ap()
    c.WU = nc.dram_tensor("WU", [INNER, C], F32, kind="ExternalInput").ap()
    c.WD = nc.dram_tensor("WD", [C, INNER], F32, kind="ExternalInput").ap()
    c.OUT = nc.dram_tensor("OUT", [TTOT, C], F32, kind="ExternalOutput").ap()
    c.GU = nc.dram_tensor("GU", [TTOT, INNER], F32, kind="Internal").ap()
    c.XPD = nc.dram_tensor("XPD", [TTOT, C], F32, kind="Internal").ap()
    c.WDQ = nc.dram_tensor("WDQ", [C, INNER], BF16, kind="Internal").ap()

    ident = np.eye(128, dtype=ml_dtypes.bfloat16)
    c.IDENT = nc.inline_tensor(ident, "IDENT").ap()
    p_i = np.arange(128)[:, None]
    j_i = np.arange(896)[None, :]
    maskw = (p_i <= j_i - 384).astype(ml_dtypes.bfloat16)
    c.MASKW = nc.inline_tensor(maskw, "MASKW").ap()

    c.b = _alloc_sbuf(nc)
    c.pA = nc.alloc_psum_tensor("pA", [128, 1024], F32)
    c.pB = nc.alloc_psum_tensor("pB", [128, 1024], F32)
    c.pC = nc.alloc_psum_tensor("pC", [128, 1024], F32)
    c.pD = nc.alloc_psum_tensor("pD", [128, 1024], F32)

    c.taps = {}
    tap_specs = {"A_sb": ([128, 8192], BF16), "V_sb": ([128, 2048], BF16),
                 "q1_fm": ([128, 4096], BF16), "ckv_lm": ([128, 4096], BF16),
                 "ckv_tm": ([128, 4096], BF16), "consts": ([128, 8], F32),
                 "M_sb": ([128, 2048], BF16), "WdqT": ([128, 2048], BF16),
                 "xp0": ([128, 16384], F32), "codes1": ([128, 32768], BF16),
                 "d1g": ([128, 32], F32), "rmax2": ([128, 32], F32),
                 "accw": ([128, 4], F32)}
    for t_ in taps:
        shp, dt_ = tap_specs[t_]
        c.taps[t_] = nc.dram_tensor("tap_" + t_, shp, dt_,
                                    kind="ExternalOutput").ap()

    import contextlib
    stack = contextlib.ExitStack()
    sem_handles = [(nm, stack.enter_context(nc.semaphore(nm)))
                   for nm in ("sSY", "sSC", "sGP", "sPE", "sDV", "sAC", "sPL")]

    with nc.Block() as blk:
        def run_pass(engine_name):
            def body(real):
                for nm, h in sem_handles:
                    setattr(c, nm, S(h))
                c.sy = E(real if engine_name == "sync" else None)
                c.pe = E(real if engine_name == "tensor" else None)
                c.ve = E(real if engine_name == "vector" else None)
                c.ac = E(real if engine_name == "scalar" else None)
                c.gp = E(real if engine_name == "gpsimd" else None)
                program(c)
            return body
        blk.sync(run_pass("sync"))
        blk.tensor(run_pass("tensor"))
        blk.vector(run_pass("vector"))
        blk.scalar(run_pass("scalar"))
        blk.gpsimd(run_pass("gpsimd"))
    stack.close()
    return nc



def program(c):
    p0_consts(c)
    p0_weights(c)
    for s in range(NSEQ):
        p1_proj(c, s)
        p2_attn(c, s, hook=(absmean_hook(c) if s == 0 else None))
        if s == 0 and "xp0" in c.taps:
            c.sy.wait(c.sDV)
            c.sy.op("dma_start", out=c.taps["xp0"], in_=c.b["xp"][:, :],
                    inc16=c.sSY)
        if s == NSEQ - 1:
            p4_preload(c)
        p3_codes1(c, s)
    p4_gateup(c)
    p5_down(c)
    c.sy.wait(c.sSC)
    c.sy.wait(c.sSY)
    c.sy.wait(c.sGP)


def p0_consts(c):
    b, ve, sy = c.b, c.ve, c.sy
    sy.op("dma_start", out=b["ident"][:, :], in_=c.IDENT[:, :], inc16=c.sSY)
    sy.op("dma_start", out=b["maskW"][:, :], in_=c.MASKW[:, :], inc16=c.sSY)
    ve.op("memset", b["ones_bf"][:, :], 1.0)
    ve.op("memset", b["onesr_f"][:, :], 1.0)
    ve.op("memset", b["ones128_f"][:, :], 1.0)
    ve.op("memset", b["epsb"][:, :], 1e-5)
    ve.op("memset", b["dD"][:, :], 1.0, inc=c.sDV)
    ve.wait(c.sDV)
    ve.op("tensor_scalar", b["dD"][:, :], b["dD"][:, :], M64, -M64,
          ALU.add, ALU.add, inc=c.sDV)
    ve.wait(c.sDV)
    ve.op("tensor_scalar", b["mqt"][:, :], b["dD"][:, :], M64 - M32, M32,
          ALU.mult, ALU.add, inc=c.sDV)
    ve.wait(c.sDV)
    ve.op("tensor_scalar_mul", b["nmqt"][:, :], b["mqt"][:, :], -1.0, inc=c.sDV)
    ve.wait(c.sDV)


def _transpose_group(c, srcs, psum, dst):
    pe, ve = c.pe, c.ve
    n = len(srcs)
    for k in range(n):
        pe.op("matmul", psum[:, k * 128:(k + 1) * 128], srcs[k],
              c.b["ident"][:, :], start=True, stop=True,
              inc=(c.sPE if k == n - 1 else None))
    ve.wait(c.sPE)
    ve.op("tensor_copy", dst, psum[:, 0:n * 128], inc=c.sDV)
    return c.sDV.n


def p0_weights(c):
    b, pe, ve, gp = c.b, c.pe, c.ve, c.gp
    ld_snap = {}
    for nm, src in [("Wdq_nat", c.Wdq), ("Wdkv_nat", c.Wdkv),
                    ("Wo_nat", c.Wo), ("Wuq_nat", c.Wuq),
                    ("Wuk_nat", c.Wuk), ("Wuv_nat", c.Wuv)]:
        a = src.shape[0] // 128
        gp.op("dma_start",
              out=b[nm].ap().rearrange("p (a cc) -> p a cc", a=a),
              in_=src.rearrange("(a p) cc -> p a cc", p=128), inc16=c.sGP)
        ld_snap[nm] = c.sGP.n
    pe.wait(c.sSY, 32)
    for i in range(48):
        pe.op("matmul", c.pD[:, (i % 8) * 128:(i % 8) * 128 + 128],
              b["ident"][:, :], b["ident"][:, :], start=True, stop=True)

    groups = []
    for cc in range(8):
        groups.append(([b["Wdq_nat"][:, a * 1024 + cc * 128: a * 1024 + (cc + 1) * 128]
                        for a in range(2)], b["WdqT"][:, cc * 256:(cc + 1) * 256]))
    for cc in range(8):
        groups.append(([b["Wdkv_nat"][:, a * 1024 + cc * 128: a * 1024 + (cc + 1) * 128]
                        for a in range(2)], b["WdkvT"][:, cc * 256:(cc + 1) * 256]))
    for cc in range(8):
        groups.append(([b["Wo_nat"][:, oc * 1024 + cc * 128: oc * 1024 + (cc + 1) * 128]
                        for oc in range(8)], b["WoT"][:, cc * 1024:(cc + 1) * 1024]))
    ps = [c.pA, c.pB]
    evac_of = {-2: c.sDV.n, -1: c.sDV.n}
    for gi, (srcs, dst) in enumerate(groups):
        if gi == 0:
            pe.wait(c.sGP, ld_snap["Wdq_nat"])
        elif gi == 8:
            pe.wait(c.sGP, ld_snap["Wdkv_nat"])
        elif gi == 16:
            pe.wait(c.sGP, ld_snap["Wo_nat"])
        psum = ps[gi % 2]
        pe.wait(c.sDV, evac_of[gi - 2])
        n = len(srcs)
        for k in range(n):
            pe.op("matmul", psum[:, k * 128:(k + 1) * 128], srcs[k],
                  b["ident"][:, :], start=True, stop=True,
                  inc=(c.sPE if k == n - 1 else None))
        ve.wait(c.sPE)
        ve.op("tensor_copy", dst, psum[:, 0:n * 128], inc=c.sDV)
        evac_of[gi] = c.sDV.n

    cp = [c.pC, c.pD]
    cev = {-2: c.sDV.n, -1: c.sDV.n}
    ci = 0

    def compute_group(mms, evacs):
        nonlocal ci
        psum = cp[ci % 2]
        pe.wait(c.sDV, cev[ci - 2])
        nmm = len(mms)
        for k, (osl, lhsT, rhs, st_, sp_) in enumerate(mms):
            pe.op("matmul", osl(psum), lhsT, rhs, start=st_, stop=sp_,
                  inc=(c.sPE if k == nmm - 1 else None))
        ve.wait(c.sPE)
        nev = len(evacs)
        for k, (dst, ssl) in enumerate(evacs):
            ve.op("tensor_copy", dst, ssl(psum),
                  inc=(c.sDV if k == nev - 1 else None))
        cev[ci] = c.sDV.n
        ci += 1

    pe.wait(c.sGP, ld_snap["Wuk_nat"])
    for qc in range(2):
        mms = [(lambda p: p[:, 0:256],
                b["Wuq_nat"][:, k * 256 + qc * 128: k * 256 + qc * 128 + 128],
                b["Wuk_nat"][:, k * 256:(k + 1) * 256], k == 0, k == 7)
               for k in range(8)]
        compute_group(mms, [(b["T1_sb"][:, qc * 256:(qc + 1) * 256],
                             lambda p: p[:, 0:256])])
    pe.wait(c.sDV, cev[1])
    for cc in range(8):
        mms = [(lambda p: p[:, 0:256],
                b["Wdq_nat"][:, a * 1024 + cc * 128: a * 1024 + (cc + 1) * 128],
                b["T1_sb"][:, a * 256:(a + 1) * 256], a == 0, a == 1)
               for a in range(2)]
        compute_group(mms, [(b["M_sb"][:, cc * 256:(cc + 1) * 256],
                             lambda p: p[:, 0:256])])
    pe.wait(c.sDV, cev[9])
    for hp in range(NH // 2):
        mms = []
        for sub in range(2):
            h = 2 * hp + sub
            tile, off = h // 2, (h % 2) * 64
            for qc in range(2):
                mms.append(
                    (lambda p, qc=qc, sub=sub:
                     p[:, sub * 512 + qc * 256: sub * 512 + (qc + 1) * 256],
                     b["Wuq_nat"][off:off + 64,
                                  tile * 256 + qc * 128: tile * 256 + qc * 128 + 128],
                     b["M_sb"][off:off + 64, tile * 256:(tile + 1) * 256],
                     True, True))
        compute_group(mms, [(b["A_sb"][:, hp * 1024:(hp + 1) * 1024],
                             lambda p: p[:, 0:1024])])
    pe.wait(c.sDV, evac_of[23])
    pe.wait(c.sGP, ld_snap["Wuv_nat"])
    for lc in range(2):
        mms = []
        for ch in range(2):
            for k in range(8):
                mms.append((lambda p, ch=ch: p[:, ch * 512:(ch + 1) * 512],
                            b["Wuv_nat"][:, k * 256 + lc * 128: k * 256 + lc * 128 + 128],
                            b["WoT"][:, k * 1024 + ch * 512: k * 1024 + (ch + 1) * 512],
                            k == 0, k == 7))
        compute_group(mms, [(b["V_sb"][:, lc * 1024:(lc + 1) * 1024],
                             lambda p: p[:, 0:1024])])
    ve.wait(c.sDV)
    c.p0_pe_done = c.sPE.n
    for t_ in ("M_sb", "A_sb", "V_sb", "WdqT"):
        if t_ in c.taps:
            c.sy.wait(c.sDV)
            c.sy.op("dma_start", out=c.taps[t_], in_=b[t_][:, :], inc16=c.sSY)


def absmean_hook(c):
    b, ve, pe, gp = c.b, c.ve, c.pe, c.gp
    ve.op("memset", b["accw"][:, :], 0.0)
    ve.op("memset", b["vals"][:, :], 0.0, inc=c.sDV)
    ve.wait(c.sDV)
    mats = [(c.WG, 0), (c.WU, 1), (c.WD, 2)]
    chunks = [(src, col, k) for src, col in mats for k in range(8)]
    dve_done = []

    def emit_chunk(i):
        src, col, k = chunks[i]
        st = b["wst0"] if i % 2 == 0 else b["wst1"]
        if i >= 2:
            gp.wait(c.sDV, dve_done[i - 2])
        a_per = 4096 // src.shape[1]
        gp.op("dma_start",
              out=st.ap().rearrange("p (a cc) -> p a cc", a=a_per),
              in_=src.rearrange("(a p) cc -> p a cc", p=128)[
                  :, k * a_per:(k + 1) * a_per, :], inc16=c.sGP)
        ve.wait(c.sGP)
        ve.op("tensor_reduce", b["st_h"][:, :], st[:, :], AX.X, ALU.add,
              apply_absolute_value=True, inc=c.sDV)
        ve.wait(c.sDV)
        ve.op("tensor_tensor", b["accw"][:, col:col + 1], b["st_h"][:, :],
              b["accw"][:, col:col + 1], ALU.add, inc=c.sDV)
        dve_done.append(c.sDV.n)

    def emit_final():
        pe.wait(c.sDV)
        for col in range(3):
            pe.op("matmul", c.pA[0:1, col:col + 1], b["accw"][:, col:col + 1],
                  b["ones128_f"][:, :], start=True, stop=True,
                  inc=(c.sPE if col == 2 else None))
        ve.wait(c.sPE)
        n_elems = float(INNER * C)
        for col in range(3):
            ve.op("tensor_scalar", b["vals"][0:1, col + 3:col + 4],
                  c.pA[0:1, col:col + 1], 1.0 / n_elems, 1e-5, ALU.mult,
                  ALU.max, inc=(c.sDV if col == 2 else None))
        ve.wait(c.sDV)
        for col in range(3):
            ve.op("reciprocal", b["vals"][0:1, col:col + 1],
                  b["vals"][0:1, col + 3:col + 4])
        for col in range(3):
            ve.op("tensor_scalar_mul", b["vals"][0:1, col + 3:col + 4],
                  b["vals"][0:1, col + 3:col + 4], 1.0 / 127.0,
                  inc=(c.sDV if col == 2 else None))
        pe.wait(c.sDV)
        pe.op("matmul", c.pB[:, 0:8], b["onesr_f"][:, :], b["vals"][0:1, :],
              start=True, stop=True, inc=c.sPE)
        ve.wait(c.sPE)
        ve.op("tensor_copy", b["consts"][:, :], c.pB[:, 0:8], inc=c.sDV)
        ve.wait(c.sDV)
        pe.wait(c.sDV)
        if "consts" in c.taps:
            c.sy.wait(c.sDV)
            c.sy.op("dma_start", out=c.taps["consts"], in_=b["consts"][:, :],
                    inc16=c.sSY)
        if "accw" in c.taps:
            c.sy.wait(c.sDV)
            c.sy.op("dma_start", out=c.taps["accw"], in_=b["accw"][:, :],
                    inc16=c.sSY)

    def hook(step):
        if step < 24:
            emit_chunk(step)
        elif step == 24:
            emit_final()
    return hook


def _ln_stats2(c, xin, dump):
    b, ve, ac = c.b, c.ve, c.ac
    ve.op("tensor_reduce", b["st_a"][:, :], xin, AX.X, ALU.add, inc=c.sDV)
    ac.wait(c.sDV)
    ac.op("activation", dump, xin, AF.Square, accum_out=b["st_c"][:, 0:1],
          inc=c.sAC)
    ve.wait(c.sAC)
    ve.wait(c.sDV)
    ve.op("tensor_scalar_mul", b["st_b"][:, :], b["st_a"][:, :], 1.0 / 1024.0)
    ve.op("scalar_tensor_tensor", b["st_d"][:, :], b["st_a"][:, :],
          1.0 / (1024.0 * 1024.0), b["st_a"][:, :], ALU.mult, ALU.mult,
          inc=c.sDV)
    ve.wait(c.sDV)
    ve.op("scalar_tensor_tensor", b["st_d"][:, :], b["st_c"][:, :],
          1.0 / 1024.0, b["st_d"][:, :], ALU.mult, ALU.subtract, inc=c.sDV)
    ac.wait(c.sDV)
    ac.op("activation", b["st_d"][:, :], b["st_d"][:, :], AF.Sqrt,
          bias=b["epsb"][:, 0:1], inc=c.sAC)
    ve.wait(c.sAC)
    ve.op("reciprocal", b["st_e"][:, :], b["st_d"][:, :], inc=c.sDV)
    ve.wait(c.sDV)


def p1_proj(c, s):
    b, pe, ve, ac, sy = c.b, c.pe, c.ve, c.ac, c.sy
    if s > 0:
        half_snap = [c.xp1_ld[7], c.xp1_ld[15]]
    else:
        sy.wait(c.sPE, c.p0_pe_done)
        half_snap = []
        for hf in range(2):
            sy.op("dma_start",
                  out=b["xp"].ap().rearrange("p (i cc) -> p i cc", i=NT)[
                      :, hf * 8:(hf + 1) * 8, :],
                  in_=c.X[s * T + hf * 1024: s * T + (hf + 1) * 1024, :]
                      .rearrange("(i p) cc -> p i cc", p=128),
                  inc16=c.sSY)
            half_snap.append(c.sSY.n)

    SUMq = b["hstat"][:, 0:4]
    SQSq = b["hstat"][:, 4:8]
    MEAN16 = b["hstat"][:, 20:36]
    RSTD16 = b["hstat"][:, 36:52]

    def stats_add(q, i):
        if i == 0:
            ve.wait(c.sSY, half_snap[q // 2])
            ac.wait(c.sSY, half_snap[q // 2])
            ac.wait(c.sDV)
        gi = 4 * q + i
        xpt = b["xp"][:, gi * 1024:(gi + 1) * 1024]
        ve.op("tensor_reduce", b["hstat"][:, i:i + 1], xpt, AX.X, ALU.add,
              inc=(c.sDV if i == 3 else None))
        ac.op("activation", b["sq_scr"][:, :], xpt, AF.Square,
              accum_out=b["hstat"][:, 4 + i:5 + i],
              inc=(c.sAC if i == 3 else None))

    def stats_fin(q):
        ve.wait(c.sAC)
        ve.wait(c.sDV)
        ve.op("tensor_scalar_mul", MEAN16[:, 4 * q:4 * q + 4], SUMq, 1.0 / 1024.0)
        ve.op("scalar_tensor_tensor", SUMq, SUMq, 1.0 / (1024.0 * 1024.0), SUMq,
              ALU.mult, ALU.mult, inc=c.sDV)
        ve.wait(c.sDV)
        ve.op("scalar_tensor_tensor", SQSq, SQSq, 1.0 / 1024.0, SUMq,
              ALU.mult, ALU.subtract, inc=c.sDV)
        ac.wait(c.sDV)
        ac.op("activation", SQSq, SQSq, AF.Sqrt, bias=b["epsb"][:, 0:1],
              inc=c.sAC)
        ve.wait(c.sAC)
        ve.op("reciprocal", RSTD16[:, 4 * q:4 * q + 4], SQSq, inc=c.sDV)
        ve.wait(c.sDV)

    tmb = [b["tm_bf"], b["ctx_sb"]]
    transp_done = [c.sPE.n, c.sPE.n]
    apply_snap = {}
    evac_snap = {-2: c.sDV.n, -1: c.sDV.n}
    q1ckv_done = {-1: c.sPE.n}

    def apply_t(gi):
        xpt = b["xp"][:, gi * 1024:(gi + 1) * 1024]
        ve.wait(c.sPE, transp_done[gi % 2])
        ve.op("tensor_scalar", tmb[gi % 2][:, :], xpt,
              b["hstat"][:, 20 + gi:21 + gi], b["hstat"][:, 36 + gi:37 + gi],
              ALU.subtract, ALU.mult, inc=c.sDV)
        apply_snap[gi] = c.sDV.n

    tp_banks = [c.pA, c.pD]

    def transp(gi):
        tb, i = gi // 4, gi % 4
        psum = tp_banks[gi % 2]
        pe.wait(c.sDV, max(apply_snap[gi], evac_snap[gi - 2]))
        if i == 3:
            pe.wait(c.sDV, evac_snap[gi - 1])
        for k in range(8):
            pe.op("matmul", psum[:, k * 128:(k + 1) * 128],
                  tmb[gi % 2][:, k * 128:(k + 1) * 128], b["ident"][:, :],
                  start=True, stop=True, inc=(c.sPE if k == 7 else None))
        transp_done[gi % 2] = c.sPE.n
        ve.wait(c.sPE, transp_done[gi % 2])
        if i == 0:
            ve.wait(c.sPE, q1ckv_done[tb - 1])
        dst = b["ln_fm_blk"].ap().rearrange("p (cc t) -> p cc t", cc=8)[
            :, :, i * 128:(i + 1) * 128]
        src3 = psum.ap().rearrange("p (cc t) -> p cc t", cc=8)
        ve.op("tensor_copy", dst, src3, inc=c.sDV)
        evac_snap[gi] = c.sDV.n

    def qkv(tb):
        pe.wait(c.sDV, evac_snap[4 * tb + 3])
        for qc in range(2):
            for k in range(8):
                pe.op("matmul", c.pB[:, qc * 512:(qc + 1) * 512],
                      b["WdqT"][:, k * 256 + qc * 128: k * 256 + qc * 128 + 128],
                      b["ln_fm_blk"][:, k * 512:(k + 1) * 512],
                      start=(k == 0), stop=(k == 7),
                      inc=(c.sPE if (qc == 1 and k == 7) else None))
        ve.wait(c.sPE)
        ve.op("tensor_copy", b["q1_fm"][:, tb * 512:(tb + 1) * 512],
              c.pB[:, 0:512])
        ve.op("tensor_copy", b["q1_fm"][:, 2048 + tb * 512: 2048 + (tb + 1) * 512],
              c.pB[:, 512:1024], inc=c.sDV)
        for lc in range(2):
            for k in range(8):
                pe.op("matmul", c.pC[:, lc * 512:(lc + 1) * 512],
                      b["WdkvT"][:, k * 256 + lc * 128: k * 256 + lc * 128 + 128],
                      b["ln_fm_blk"][:, k * 512:(k + 1) * 512],
                      start=(k == 0), stop=(k == 7),
                      inc=(c.sPE if (lc == 1 and k == 7) else None))
        q1ckv_done[tb] = c.sPE.n
        ve.wait(c.sPE)
        ve.op("tensor_copy", b["ckv_lm"][:, tb * 512:(tb + 1) * 512],
              c.pC[:, 0:512])
        ve.op("tensor_copy", b["ckv_lm"][:, 2048 + tb * 512: 2048 + (tb + 1) * 512],
              c.pC[:, 512:1024], inc=c.sDV)
        ckvlm_evac = c.sDV.n
        pe.wait(c.sDV, ckvlm_evac)
        srcs = []
        for i in range(4):
            it = tb * 4 + i
            for lc in range(2):
                srcs.append(b["ckv_lm"][:, lc * 2048 + it * 128:
                                        lc * 2048 + (it + 1) * 128])
        ev = _transpose_group(c, srcs, c.pD,
                              b["ckv_tm"][:, tb * 1024:(tb + 1) * 1024])
        pe.wait(c.sDV, ev)

    for i in range(4):
        stats_add(0, i)
    stats_fin(0)
    for gi in range(NT):
        apply_t(gi)
        if gi >= 1:
            transp(gi - 1)
        if gi >= 4 and gi % 4 == 0:
            qkv(gi // 4 - 1)
        q_next = gi // 4 + 1
        if q_next < NTB:
            stats_add(q_next, gi % 4)
            if gi % 4 == 3:
                stats_fin(q_next)
    transp(NT - 1)
    qkv(NTB - 1)
    if s == 0:
        for t_ in ("q1_fm", "ckv_lm", "ckv_tm"):
            if t_ in c.taps:
                c.sy.wait(c.sDV)
                c.sy.op("dma_start", out=c.taps[t_], in_=b[t_][:, :],
                        inc16=c.sSY)


def p2_attn(c, s, hook=None):
    b, pe, ve, ac = c.b, c.pe, c.ve, c.ac
    sc_banks = [c.pA[:, 0:512], c.pA[:, 512:1024], c.pB[:, 0:512]]
    cx = [c.pB[:, 512:1024], c.pC[:, 0:512]]
    sm = c.pC[:, 512:1024]
    ql = [c.pD[:, 0:512], c.pD[:, 512:1024]]
    yb = c.pD[:, 0:256]
    scols = c.pD[:, 512:516]

    exp_of_bank = [c.sAC.n] * 3
    pt_consumed = [c.sPE.n] * 3
    proj_done = c.sDV.n
    qsb = [b["qlat_sb"], b["qlat_sb2"]]

    st = {
        "scores_done": [c.sPE.n, c.sPE.n],
        "y_done": c.sPE.n,
        "yevac": c.sDV.n,
        "qlat_evac": {},
        "ctx_evac": c.sDV.n,
        "rsum_copy": c.sDV.n,
    }
    heads = [(Tb, h) for Tb in range(NTB) for h in range(NH)]
    pending = None
    p3q_pending = []
    if s == 0:
        c.c1q_done = {}
        c.xp1_ld = {}

    def emit_qlat(hh_):
        if hh_ >= len(heads):
            return
        Tb, h = heads[hh_]
        pe.wait(c.sDV, max(proj_done, st["yevac"]))
        for lc in range(2):
            for qc in range(2):
                pe.op("matmul", ql[lc],
                      b["A_sb"][:, h * 512 + qc * 256 + lc * 128:
                                h * 512 + qc * 256 + (lc + 1) * 128],
                      b["q1_fm"][:, qc * 2048 + Tb * 512:
                                 qc * 2048 + (Tb + 1) * 512],
                      start=(qc == 0), stop=(qc == 1),
                      inc=(c.sPE if (lc == 1 and qc == 1) else None))
        snap_ql = c.sPE.n
        ve.wait(c.sPE, max(snap_ql, st["scores_done"][hh_ % 2]))
        ve.op("tensor_copy", qsb[hh_ % 2][:, 0:512], ql[0])
        ve.op("tensor_copy", qsb[hh_ % 2][:, 512:1024], ql[1], inc=c.sDV)
        st["qlat_evac"][hh_] = c.sDV.n

    def emit_tail(Tb_, h_, hh_):
        ve.wait(c.sPE, st["y_done"])
        ve.op("tensor_copy", b["ctx_sb"][:, 0:512], cx[0])
        ve.op("tensor_copy", b["ctx_sb"][:, 512:1024], cx[1], inc=c.sDV)
        st["ctx_evac"] = c.sDV.n
        pe.wait(c.sDV, max(st["rsum_copy"], st["qlat_evac"][hh_ + 1]
                           if (hh_ + 1) in st["qlat_evac"] else st["rsum_copy"]))
        for k in range(4):
            pe.op("matmul", scols[:, k:k + 1],
                  b["rsum"][0:1, k * 128:(k + 1) * 128],
                  b["onesr_f"][0:1, 0:1], start=True, stop=True,
                  inc=(c.sPE if k == 3 else None))
        snap_scols = c.sPE.n
        ve.wait(c.sPE, snap_scols)
        ve.op("reciprocal", b["recipc"][:, 0:4], scols, inc=c.sDV)
        ve.wait(c.sDV)
        pe.wait(c.sDV, st["ctx_evac"])
        for k in range(4):
            for lc in range(2):
                pe.op("matmul", yb[:, k * 64:(k + 1) * 64],
                      b["ctx_sb"][:, lc * 512 + k * 128:
                                  lc * 512 + (k + 1) * 128],
                      b["V_sb"][:, lc * 1024 + h_ * 64:
                                lc * 1024 + (h_ + 1) * 64],
                      start=(lc == 0), stop=(lc == 1),
                      inc=(c.sPE if (k == 3 and lc == 1) else None))
        st["y_done"] = c.sPE.n
        ve.wait(c.sPE, st["y_done"])
        for k in range(4):
            xpv = b["xp"][:, (Tb_ * 4 + k) * 1024 + h_ * 64:
                          (Tb_ * 4 + k) * 1024 + (h_ + 1) * 64]
            ve.op("scalar_tensor_tensor", xpv, yb[:, k * 64:(k + 1) * 64],
                  b["recipc"][:, k:k + 1], xpv, ALU.mult, ALU.add,
                  inc=(c.sDV if k == 3 else None))
        st["yevac"] = c.sDV.n
        if h_ == NH - 1:
            ac.wait(c.sDV, st["yevac"])
            ac.op("dma_start",
                  out=c.XPD[s * T + Tb_ * 512: s * T + (Tb_ + 1) * 512, :]
                      .rearrange("(i p) cc -> p i cc", p=128),
                  in_=b["xp"].ap().rearrange("p (i cc) -> p i cc", i=NT)[
                      :, Tb_ * 4:(Tb_ + 1) * 4, :], inc16=c.sSC)
            p3q_pending.extend(p3_quarter_slices(c, s, Tb_))

    emit_qlat(0)
    for hh, (Tb, h) in enumerate(heads):
        nj = 4 * Tb + 4
        if pending is not None:
            emit_tail(*pending)
        if p3q_pending:
            p3q_pending.pop(0)()
        if hook is not None:
            hook(hh)
        pending = (Tb, h, hh)
        snap_exp = {}
        snap_mask = {}
        snap_scores = {}

        def emit_scores(j):
            bk = j % 3
            pe.wait(c.sAC, exp_of_bank[bk])
            if j == 0:
                pe.wait(c.sDV, st["qlat_evac"][hh])
            for lc in range(2):
                pe.op("matmul", sc_banks[bk],
                      b["ckv_lm"][:, lc * 2048 + j * 128:
                                  lc * 2048 + (j + 1) * 128],
                      qsb[hh % 2][:, lc * 512:(lc + 1) * 512],
                      start=(lc == 0), stop=(lc == 1),
                      inc=(c.sPE if lc == 1 else None))
            snap_scores[j] = c.sPE.n
            ptb = j % 3
            pt = b["PT%d" % ptb]
            ac.wait(c.sPE, max(snap_scores[j], pt_consumed[ptb]))
            ac.op("activation", pt[:, :], sc_banks[bk], AF.Exp, scale=0.125,
                  inc=c.sAC)
            snap_exp[j] = c.sAC.n
            exp_of_bank[bk] = snap_exp[j]
            if j >= 4 * Tb:
                dlt = (j - 4 * Tb) * 128
                ve.wait(c.sAC, snap_exp[j])
                ve.op("tensor_tensor", pt[:, :], pt[:, :],
                      b["maskW"][:, 384 - dlt:896 - dlt], ALU.mult,
                      inc=c.sDV)
                snap_mask[j] = c.sDV.n

        def emit_ctx(j):
            ptb = j % 3
            pt = b["PT%d" % ptb]
            if j in snap_mask:
                pe.wait(c.sDV, snap_mask[j])
            else:
                pe.wait(c.sAC, snap_exp[j])
            if j == 0:
                pe.wait(c.sDV, max(st["ctx_evac"], st["rsum_copy"]))
            for lc in range(2):
                pe.op("matmul", cx[lc],
                      b["ckv_tm"][:, j * 256 + lc * 128:
                                  j * 256 + (lc + 1) * 128],
                      pt[:, :], start=(j == 0), stop=(j == nj - 1))
            pe.op("matmul", sm, b["ones_bf"][:, 0:128], pt[:, :],
                  start=(j == 0), stop=(j == nj - 1), inc=c.sPE)
            pt_consumed[ptb] = c.sPE.n

        for j in range(nj):
            emit_scores(j)
            if j == 1:
                emit_qlat(hh + 1)
            if j >= 2:
                emit_ctx(j - 2)
        emit_ctx(nj - 2)
        emit_ctx(nj - 1)
        st["scores_done"][hh % 2] = snap_scores[nj - 1]
        snap_ctx_done = c.sPE.n
        ve.wait(c.sPE, snap_ctx_done)
        ve.op("tensor_copy", b["rsum"][0:1, :], sm[0:1, :], inc=c.sDV)
        st["rsum_copy"] = c.sDV.n
    emit_tail(*pending)
    c.attn_tail_pe = c.sPE.n
    c.attn_tail_dv = c.sDV.n
    for f in p3q_pending:
        f()
    p3q_pending.clear()


def p3_quarter_slices(c, s, Tb):
    b, ve, ac = c.b, c.ve, c.ac
    SUMq = b["hstat"][:, 0:4]
    SQSq = b["hstat"][:, 4:8]
    MEANq = b["hstat"][:, 8:12]
    RSTDq = b["hstat"][:, 12:16]
    R1q = b["hstat"][:, 16:20]
    base = 4 * Tb

    def s0():
        ve.wait(c.sDV)
        ac.wait(c.sDV)
        for i in range(4):
            gi = base + i
            xpt = b["xp"][:, gi * 1024:(gi + 1) * 1024]
            ve.op("tensor_reduce", b["hstat"][:, i:i + 1], xpt, AX.X, ALU.add,
                  inc=(c.sDV if i == 3 else None))
            ac.op("activation", b["ln_scr"][:, :], xpt, AF.Square,
                  accum_out=b["hstat"][:, 4 + i:5 + i],
                  inc=(c.sAC if i == 3 else None))

    def s1():
        ve.wait(c.sAC)
        ve.wait(c.sDV)
        ve.op("tensor_scalar_mul", MEANq, SUMq, 1.0 / 1024.0)
        ve.op("scalar_tensor_tensor", SUMq, SUMq, 1.0 / (1024.0 * 1024.0),
              SUMq, ALU.mult, ALU.mult, inc=c.sDV)
        ve.wait(c.sDV)
        ve.op("scalar_tensor_tensor", SQSq, SQSq, 1.0 / 1024.0, SUMq,
              ALU.mult, ALU.subtract, inc=c.sDV)
        ac.wait(c.sDV)
        ac.op("activation", SQSq, SQSq, AF.Sqrt, bias=b["epsb"][:, 0:1],
              inc=c.sAC)
        ve.wait(c.sAC)
        ve.op("reciprocal", RSTDq, SQSq, inc=c.sDV)
        ve.wait(c.sDV)

    def s2():
        for i in range(4):
            gi = base + i
            xpt = b["xp"][:, gi * 1024:(gi + 1) * 1024]
            ve.op("tensor_scalar", b["ln_scr"][:, :], xpt,
                  b["hstat"][:, 8 + i:9 + i], b["hstat"][:, 12 + i:13 + i],
                  ALU.subtract, ALU.mult)
            ve.op("tensor_reduce", b["hstat"][:, 16 + i:17 + i],
                  b["ln_scr"][:, :], AX.X, ALU.max, apply_absolute_value=True,
                  inc=(c.sDV if i == 3 else None))
        ve.wait(c.sDV)

    store_snap = c.sSC.n

    def s3():
        ve.op("tensor_scalar_max", R1q, R1q, 1e-5, inc=c.sDV)
        ve.wait(c.sDV)
        tt0 = s * NT + base
        ve.op("tensor_copy", b["rmax1"][:, tt0:tt0 + 4], R1q)
        ve.op("reciprocal", SUMq, R1q, inc=c.sDV)
        ve.wait(c.sDV)
        ve.op("tensor_scalar_mul", SUMq, SUMq, 127.0, inc=c.sDV)
        ve.wait(c.sDV)
        ve.op("tensor_tensor", b["hstat"][:, 20 + base:20 + base + 4], RSTDq,
              SUMq, ALU.mult, inc=c.sDV)
        ve.wait(c.sDV)
        ve.op("scalar_tensor_tensor", b["hstat"][:, 36 + base:36 + base + 4],
              MEANq, -1.0, b["hstat"][:, 20 + base:20 + base + 4],
              ALU.mult, ALU.mult, inc=c.sDV)
        ve.wait(c.sDV)

    def mk_q(i):
        def qz():
            gi = base + i
            tt = s * NT + gi
            xpt = b["xp"][:, gi * 1024:(gi + 1) * 1024]
            ve.op("tensor_scalar", b["ln_scr"][:, :], xpt,
                  b["hstat"][:, 20 + gi:21 + gi],
                  b["hstat"][:, 36 + gi:37 + gi], ALU.mult, ALU.add,
                  inc=c.sDV)
            xp_free = c.sDV.n
            ve.op("tensor_scalar", b["codes1"][:, tt * 1024:(tt + 1) * 1024],
                  b["ln_scr"][:, :], b["mqt"][:, 0:1], b["nmqt"][:, 0:1],
                  ALU.add, ALU.add, inc=c.sDV)
            c.c1q_done[tt] = c.sDV.n
            if s == 0:
                c.sy.wait(c.sDV, xp_free)
                c.sy.wait(c.sSC, store_snap)
                c.sy.op("dma_start",
                        out=b["xp"].ap().rearrange(
                            "p (i cc) -> p i cc", i=NT)[:, gi:gi + 1, :],
                        in_=c.X[T + gi * 128: T + (gi + 1) * 128, :]
                            .rearrange("(i p) cc -> p i cc", p=128),
                        inc16=c.sSY)
                c.xp1_ld[gi] = c.sSY.n
        return qz

    return [s0, s1, s2, s3, mk_q(0), mk_q(1), mk_q(2), mk_q(3)]


def p3_codes1(c, s):
    b, pe, ve, ac = c.b, c.pe, c.ve, c.ac
    ev = {-2: c.sAC.n, -1: c.sAC.n}
    ps = [c.pA, c.pB]
    c.c1t_done = None
    for gi in range(NT):
        tt = s * NT + gi
        psum = ps[gi % 2]
        pe.wait(c.sDV, c.c1q_done[tt])
        pe.wait(c.sAC, ev[gi - 2])
        for k in range(8):
            pe.op("matmul", psum[:, k * 128:(k + 1) * 128],
                  b["codes1"][:, tt * 1024 + k * 128: tt * 1024 + (k + 1) * 128],
                  b["ident"][:, :], start=True, stop=True,
                  inc=(c.sPE if k == 7 else None))
        ac.wait(c.sPE)
        ac.op("activation", b["codes1"][:, tt * 1024:(tt + 1) * 1024],
              psum[:, 0:1024], AF.Copy, inc=c.sAC)
        ev[gi] = c.sAC.n
    c.c1t_done = c.sAC.n
    if s == NSEQ - 1:
        for t_ in ("codes1", "d1g"):
            if t_ in c.taps:
                c.sy.wait(c.sDV)
                c.sy.op("dma_start", out=c.taps[t_], in_=b[t_][:, :],
                        inc16=c.sSY)


def p4_preload(c):
    b = c.b
    c.sy.wait(c.sPE)
    c.sy.wait(c.sDV)
    c.p4_ld = {}
    for m_i, src in enumerate([c.WG, c.WU]):
        c.sy.op("dma_start",
                out=[b["f_wst0"], b["f_wst1"]][m_i].ap().rearrange(
                    "p (a cc) -> p a cc", a=4),
                in_=src[0:512, :].rearrange("(a p) cc -> p a cc", p=128),
                inc16=c.sSY)
        c.p4_ld[(0, m_i)] = c.sSY.n


def p4_gateup(c):
    b, pe, ve, ac = c.b, c.pe, c.ve, c.ac
    ve.wait(c.sDV)
    ve.op("tensor_scalar", b["d1g"][:, :], b["rmax1"][:, :],
          b["consts"][:, 3:4], None, ALU.mult, ALU.bypass)
    ve.op("tensor_scalar", b["d1u"][:, :], b["rmax1"][:, :],
          b["consts"][:, 4:5], None, ALU.mult, ALU.bypass, inc=c.sDV)
    ve.wait(c.sDV)
    codes_ready = c.sDV.n
    mg = [b["mgT0"], b["mgT1"]]
    mu = [b["muT0"], b["muT1"]]
    wst = [b["f_wst0"], b["f_wst1"]]
    gu = [b["gu0"], b["gu1"]]
    gate_b = [c.pA[:, 0:512], c.pB[:, 0:512]]
    up_b = [c.pA[:, 512:1024], c.pB[:, 512:1024]]
    wq_used = {-2: c.sDV.n, -1: c.sDV.n}
    ld = dict(c.p4_ld)
    gu_store_of_buf = [c.sSC.n, c.sSC.n]
    dve_of_bank = [codes_ready, codes_ready]
    wt_consumed = {-2: c.sPE.n, -1: c.sPE.n}
    wt_ready = {}
    sg_consumed = c.sDV.n
    c.sy.wait(c.sSC)
    wdp_q = {-1: c.sDV.n}
    wdp_ld = {}
    c.wdp_store = {}

    def wdp(w, tt):
        if w >= 8:
            return
        st_ = b["p4wd_st"]
        dq_ = b["p4wd_dq"]
        if tt == 0:
            c.sy.wait(c.sDV, wdp_q[w - 1])
            c.sy.op("dma_start", out=st_[:, :],
                    in_=c.WD[w * 128:(w + 1) * 128, :], inc16=c.sSY)
            wdp_ld[w] = c.sSY.n
        elif 2 <= tt <= 9:
            p = tt - 2
            sl = slice(p * 512, (p + 1) * 512)
            if p == 0:
                ve.wait(c.sSY, wdp_ld[w])
            ve.op("tensor_scalar", st_[:, sl], st_[:, sl],
                  b["consts"][:, 2:3], b["mqt"][:, 0:1], ALU.mult, ALU.add)
        elif 10 <= tt <= 17:
            p = tt - 10
            sl = slice(p * 512, (p + 1) * 512)
            ve.op("tensor_scalar", st_[:, sl], st_[:, sl],
                  b["nmqt"][:, 0:1], 1.0, ALU.add, ALU.min)
        elif 18 <= tt <= 25:
            p = tt - 18
            sl = slice(p * 512, (p + 1) * 512)
            if p == 0:
                ve.wait(c.sSC, c.wdp_store.get(w - 1, 0))
            ve.op("tensor_scalar_max", dq_[:, sl], st_[:, sl], -1.0,
                  inc=(c.sDV if p == 7 else None))
            if p == 7:
                wdp_q[w] = c.sDV.n
        elif tt == 27:
            c.sy.wait(c.sDV, wdp_q[w])
            c.sy.op("dma_start", out=c.WDQ[w * 128:(w + 1) * 128, :],
                    in_=dq_[:, :], inc16=c.sSC)
            c.wdp_store[w] = c.sSC.n

    def load_ws(os_):
        if os_ >= 8:
            return
        if os_ < 2:
            c.sy.wait(c.sPE)
            c.sy.wait(c.sDV)
        for m_i, src in enumerate([c.WG, c.WU]):
            c.sy.wait(c.sDV, wq_used[os_ - 1])
            c.sy.op("dma_start",
                    out=wst[m_i].ap().rearrange("p (a cc) -> p a cc", a=4),
                    in_=src[os_ * 512:(os_ + 1) * 512, :].rearrange(
                        "(a p) cc -> p a cc", p=128), inc16=c.sSY)
            ld[(os_, m_i)] = c.sSY.n

    def prep_quant(os_, m_i, part):
        if os_ >= 8:
            return
        sl = slice(part * 512, (part + 1) * 512)
        if part == 0:
            ve.wait(c.sSY, ld[(os_, m_i)])
            ve.wait(c.sPE, wt_consumed[os_ - 2])
            ve.wait(c.sPE)
        ve.op("tensor_scalar", b["f_qscr"][:, sl], wst[m_i][:, sl],
              b["consts"][:, m_i:m_i + 1], b["mqt"][:, 0:1],
              ALU.mult, ALU.add)
        ve.op("tensor_scalar", b["f_qscr"][:, sl], b["f_qscr"][:, sl],
              b["nmqt"][:, 0:1], 1.0, ALU.add, ALU.min)
        ve.op("tensor_scalar_max", b["f_wq"][:, sl], b["f_qscr"][:, sl],
              -1.0, inc=(c.sDV if part == 7 else None))
        if part == 7 and m_i == 1:
            wq_used[os_] = c.sDV.n

    pp_evac = {-2: c.sAC.n, -1: c.sAC.n}
    pp_ps = [c.pC, c.pD]
    pp_ctr = [0]

    def prep_transp(os_, m_i, half):
        if os_ >= 8:
            return
        buf = os_ % 2
        dstT = (mg if m_i == 0 else mu)[buf]
        pe.wait(c.sDV)
        for cc in range(half * 4, half * 4 + 4):
            gk = pp_ctr[0]
            pp_ctr[0] += 1
            psum = pp_ps[gk % 2]
            pe.wait(c.sAC, pp_evac[gk - 2])
            srcs = [b["f_wq"][:, a * 1024 + cc * 128: a * 1024 + (cc + 1) * 128]
                    for a in range(4)]
            for k in range(4):
                pe.op("matmul", psum[:, k * 128:(k + 1) * 128], srcs[k],
                      b["ident"][:, :], start=True, stop=True,
                      inc=(c.sPE if k == 3 else None))
            ac.wait(c.sPE)
            ac.op("activation", dstT[:, cc * 512:(cc + 1) * 512],
                  psum[:, 0:512], AF.Copy, inc=c.sAC)
            pp_evac[gk] = c.sAC.n
        if half == 1:
            wt_ready[(os_, m_i)] = c.sAC.n

    load_ws(0)
    for part in range(8):
        prep_quant(0, 0, part)
    prep_transp(0, 0, 0)
    prep_transp(0, 0, 1)
    for part in range(8):
        prep_quant(0, 1, part)
    prep_transp(0, 1, 0)
    prep_transp(0, 1, 1)
    load_ws(1)
    for os_ in range(8):
        buf = os_ % 2
        pe.wait(c.sAC, wt_ready[(os_, 1)])
        if os_ == 0:
            pe.wait(c.sDV, codes_ready)
            pe.wait(c.sAC, c.c1t_done)
        last_u = c.sPE.n
        for tt in range(NTT):
            pb = tt % 2
            gb, ub = gate_b[pb], up_b[pb]
            pe.wait(c.sDV, dve_of_bank[pb])
            for cc in range(8):
                pe.op("matmul", gb,
                      b["codes1"][:, tt * 1024 + cc * 128:
                                  tt * 1024 + (cc + 1) * 128],
                      mg[buf][:, cc * 512:(cc + 1) * 512],
                      start=(cc == 0), stop=(cc == 7),
                      inc=(c.sPE if cc == 7 else None))
            snap_g = c.sPE.n
            for cc in range(8):
                pe.op("matmul", ub,
                      b["codes1"][:, tt * 1024 + cc * 128:
                                  tt * 1024 + (cc + 1) * 128],
                      mu[buf][:, cc * 512:(cc + 1) * 512],
                      start=(cc == 0), stop=(cc == 7),
                      inc=(c.sPE if cc == 7 else None))
            snap_u = c.sPE.n
            last_u = snap_u
            ac.wait(c.sPE, snap_g)
            ac.wait(c.sDV, sg_consumed)
            ac.op("activation", b["sg_sb"][:, :], gb, AF.Sigmoid,
                  scale=b["d1g"][:, tt:tt + 1], inc=c.sAC)
            ve.wait(c.sAC)
            gub = gu[pb]
            ve.wait(c.sSC, gu_store_of_buf[pb])
            ve.op("scalar_tensor_tensor", b["t1_sb"][:, :], gb,
                  b["d1g"][:, tt:tt + 1], b["sg_sb"][:, :], ALU.mult, ALU.mult)
            sg_consumed = c.sDV.n + 1
            ve.wait(c.sPE, snap_u)
            ve.op("tensor_tensor", gub[:, :], b["t1_sb"][:, :], ub, ALU.mult,
                  inc=c.sDV)
            dve_of_bank[pb] = c.sDV.n
            ve.op("tensor_reduce", b["rmax8"][:, os_ * 32 + tt:os_ * 32 + tt + 1],
                  gub[:, :], AX.X, ALU.max, apply_absolute_value=True)
            ac.wait(c.sDV, dve_of_bank[pb])
            ac.op("dma_start",
                  out=c.GU[tt * 128:(tt + 1) * 128, os_ * 512:(os_ + 1) * 512],
                  in_=gub[:, :], inc16=c.sSC)
            gu_store_of_buf[pb] = c.sSC.n
            if 3 <= tt <= 10:
                prep_quant(os_ + 1, 0, tt - 3)
            elif tt == 11:
                prep_transp(os_ + 1, 0, 0)
            elif tt == 13:
                prep_transp(os_ + 1, 0, 1)
            elif 19 <= tt <= 26:
                prep_quant(os_ + 1, 1, tt - 19)
            elif tt == 27:
                prep_transp(os_ + 1, 1, 0)
            elif tt == 29:
                prep_transp(os_ + 1, 1, 1)
            elif tt == 30:
                load_ws(os_ + 2)
            wdp(os_, tt)
        wt_consumed[os_] = last_u
    ve.op("engine_nop", inc=c.sDV)
    ve.wait(c.sDV)
    if "rmax2" in c.taps:
        c.sy.wait(c.sDV)
        c.sy.op("dma_start", out=c.taps["rmax2"], in_=b["rmax2"][:, :],
                inc16=c.sSY)


def p5_down(c):
    b, pe, ve, ac = c.b, c.pe, c.ve, c.ac
    gst = [b["gst0"], b["gst1"]]
    ve.wait(c.sSC)
    r8 = b["rmax8"]
    ve.op("tensor_tensor", r8[:, 0:64], r8[:, 0:64], r8[:, 64:128], ALU.max,
          inc=c.sDV)
    ve.wait(c.sDV)
    ve.op("tensor_tensor", r8[:, 0:32], r8[:, 0:32], r8[:, 32:64], ALU.max,
          inc=c.sDV)
    ve.wait(c.sDV)
    ve.op("tensor_tensor", b["rmax2"][:, :], r8[:, 0:32],
          r8[:, 128:160], ALU.max, inc=c.sDV)
    ve.wait(c.sDV)
    ve.op("tensor_tensor", b["rmax2"][:, :], b["rmax2"][:, :],
          r8[:, 160:192], ALU.max, inc=c.sDV)
    ve.wait(c.sDV)
    ve.op("tensor_tensor", b["rmax2"][:, :], b["rmax2"][:, :],
          r8[:, 192:224], ALU.max, inc=c.sDV)
    ve.wait(c.sDV)
    ve.op("tensor_tensor", b["rmax2"][:, :], b["rmax2"][:, :],
          r8[:, 224:256], ALU.max, inc=c.sDV)
    ve.wait(c.sDV)
    wdq_st = [b["wdq_st0"], b["wdq_st1"]]
    wdq_ld = {}
    wd_tpe = {}
    g_ctr = [0]
    g_hist = {}

    def wdq_load(w):
        if w >= 8:
            return
        if w < 2:
            c.sy.wait(c.sPE)
        else:
            c.sy.wait(c.sPE, wd_tpe[w - 2])
        c.sy.op("dma_start", out=wdq_st[w % 2][:, :],
                in_=c.WDQ[w * 128:(w + 1) * 128, :], inc16=c.sSY)
        wdq_ld[w] = c.sSY.n

    down_b = [c.pA, c.pB]
    xst = [b["xpst0"], b["xpst1"]]
    ost = [b["outst0"], b["outst1"]]
    c2tm = [b["codes2_tm"], b["codes2_tm1"]]
    c2fm = [b["codes2_fm"], b["codes2_fm1"]]
    gu_ld = {}
    xp_ld = {}
    quant_done_of_buf = [c.sDV.n, c.sDV.n]
    tm_consumed = [c.sPE.n, c.sPE.n]
    fm_consumed = [c.sPE.n, c.sPE.n]
    out_store_of_buf = [c.sSC.n, c.sSC.n]
    down_evac_of_buf = [c.sDV.n, c.sDV.n]
    down_done = {}

    def prefetch_gu(t):
        q = t % 2
        c.sy.wait(c.sDV, quant_done_of_buf[q])
        c.sy.op("dma_start", out=gst[q][:, :],
                in_=c.GU[t * 128:(t + 1) * 128, :], inc16=c.sSY)
        gu_ld[t] = c.sSY.n

    def prefetch_xp(t):
        q = t % 2
        c.sy.wait(c.sDV, down_evac_of_buf[q])
        c.sy.op("dma_start", out=xst[q][:, :],
                in_=c.XPD[t * 128:(t + 1) * 128, :], inc16=c.sSC)
        xp_ld[t] = c.sSC.n

    def emit_quant(tt):
        pb = tt % 2
        ve.wait(c.sSY, gu_ld[tt])
        ve.op("tensor_tensor", b["st_a"][:, :], b["rmax2"][:, tt:tt + 1],
              b["d1u"][:, tt:tt + 1], ALU.mult, inc=c.sDV)
        ve.wait(c.sDV)
        ve.op("tensor_scalar_max", b["st_a"][:, :], b["st_a"][:, :], 1e-5,
              inc=c.sDV)
        ve.wait(c.sDV)
        ve.op("reciprocal", b["st_b"][:, :], b["st_a"][:, :], inc=c.sDV)
        ve.wait(c.sDV)
        q4 = tt % 4
        ve.op("scalar_tensor_tensor", b["recipc"][:, 4 + q4:5 + q4],
              b["st_b"][:, :], 127.0, b["d1u"][:, tt:tt + 1],
              ALU.mult, ALU.mult)
        ve.op("tensor_tensor", b["recipc"][:, q4:q4 + 1], b["st_a"][:, :],
              b["consts"][:, 5:6], ALU.mult, inc=c.sDV)
        ve.wait(c.sDV)
        ve.wait(c.sPE, tm_consumed[pb])
        ve.op("tensor_scalar", gst[pb][:, :], gst[pb][:, :],
              b["recipc"][:, 4 + q4:5 + q4], b["mqt"][:, 0:1],
              ALU.mult, ALU.add)
        ve.op("tensor_scalar", c2tm[pb][:, :], gst[pb][:, :],
              b["nmqt"][:, 0:1], None, ALU.add, ALU.bypass, inc=c.sDV)
        quant_done_of_buf[pb] = c.sDV.n

    tr_evac_g = {}
    tr_ps = [c.pC, c.pD]

    def transp8(srcs, dst, view, eng, sem):
        gi = g_ctr[0]
        g_ctr[0] += 1
        psum = tr_ps[gi % 2]
        if gi - 2 in g_hist:
            sm, cnt = g_hist[gi - 2]
            pe.wait(sm, cnt)
        for k in range(8):
            pe.op("matmul", psum[:, k * 128:(k + 1) * 128], srcs[k],
                  b["ident"][:, :], start=True, stop=True,
                  inc=(c.sPE if k == 7 else None))
        eng.wait(c.sPE)
        if eng is ac:
            eng.op("activation", dst, view(psum), AF.Copy, inc=sem)
        else:
            eng.op("tensor_copy", dst, view(psum), inc=sem)
        g_hist[gi] = (sem, sem.n)
        return (sem, sem.n)

    def wd_transp(w):
        pe.wait(c.sSY, wdq_ld[w])
        for g in range(4):
            srcs = [wdq_st[w % 2][:, (g * 8 + k) * 128:
                                  (g * 8 + k + 1) * 128] for k in range(8)]
            dst = b["mdownT"].ap().rearrange("p (oc t) -> p oc t", oc=32)[
                :, g * 8:(g + 1) * 8, w * 128:(w + 1) * 128]
            transp8(srcs, dst,
                    lambda ps: ps.ap().rearrange("p (k t) -> p k t", k=8),
                    ve, c.sDV)
        wd_tpe[w] = c.sPE.n

    def emit_transposes(tt):
        pb = tt % 2
        pe.wait(c.sDV, quant_done_of_buf[pb])
        pe.wait(c.sPE, fm_consumed[pb])
        for g in range(4):
            srcs = [c2tm[pb][:, (g * 8 + k) * 128:(g * 8 + k + 1) * 128]
                    for k in range(8)]
            tr_evac_g[(tt, g)] = transp8(
                srcs, c2fm[pb][:, g * 1024:(g + 1) * 1024],
                lambda ps: ps[:, 0:1024], ac, c.sAC)
        tm_consumed[pb] = c.sPE.n

    def emit_down(tt):
        pb = tt % 2
        pe.wait(c.sDV, down_evac_of_buf[pb])
        if tt == 0:
            pe.wait(c.sDV, c.mdown_dv)
        for g in range(4):
            sm, cnt = tr_evac_g[(tt, g)]
            pe.wait(sm, cnt)
            for oc in range(g * 8, g * 8 + 8):
                for half in range(2):
                    pe.op("matmul",
                          down_b[pb][:, half * 512:(half + 1) * 512],
                          c2fm[pb][:, oc * 128:(oc + 1) * 128],
                          b["mdownT"][:, oc * 1024 + half * 512:
                                      oc * 1024 + (half + 1) * 512],
                          start=(oc == 0), stop=(oc == 31),
                          inc=(c.sPE if (oc == 31 and half == 1) else None))
        down_done[tt] = c.sPE.n
        fm_consumed[pb] = c.sPE.n
        ve.wait(c.sPE, down_done[tt])
        ve.wait(c.sSC, max(xp_ld[tt], out_store_of_buf[pb]))
        for half in range(2):
            ve.op("scalar_tensor_tensor",
                  ost[pb][:, half * 512:(half + 1) * 512],
                  down_b[pb][:, half * 512:(half + 1) * 512],
                  b["recipc"][:, tt % 4:tt % 4 + 1],
                  xst[pb][:, half * 512:(half + 1) * 512],
                  ALU.mult, ALU.add, inc=(c.sDV if half == 1 else None))
        down_evac_of_buf[pb] = c.sDV.n
        c.sy.wait(c.sDV, down_evac_of_buf[pb])
        c.sy.op("dma_start", out=c.OUT[tt * 128:(tt + 1) * 128, :],
                in_=ost[pb][:, :], inc16=c.sSC)
        out_store_of_buf[pb] = c.sSC.n
        if tt + 1 < NTT:
            prefetch_xp(tt + 1)

    c.sy.wait(c.sSC)
    wdq_load(0)
    prefetch_gu(0)
    prefetch_xp(0)
    wdq_load(1)
    prefetch_gu(1)
    emit_quant(0)
    wd_transp(0)
    wdq_load(2)
    emit_transposes(0)
    emit_quant(1)
    wd_transp(1)
    wdq_load(3)
    emit_transposes(1)
    prefetch_gu(2)
    emit_quant(2)
    for w in range(2, 8):
        wd_transp(w)
        wdq_load(w + 2)
    c.mdown_dv = c.sDV.n
    for tt in range(NTT):
        if 2 < tt + 2 < NTT:
            prefetch_gu(tt + 2)
        if tt >= 1:
            emit_down(tt - 1)
        if tt >= 2:
            emit_transposes(tt)
        if 2 < tt + 1 < NTT:
            emit_quant(tt + 1)
    emit_down(NTT - 1)



def kernel(L0_attn_x, L0_attn_W_dq, L0_attn_W_uq, L0_attn_W_dkv, L0_attn_W_uk,
           L0_attn_W_uv, L0_attn_W_o, L0_ffn_w_gate, L0_ffn_w_up,
           L0_ffn_w_down):
    x = np.ascontiguousarray(np.asarray(L0_attn_x, dtype=np.float32))
    w = {
        "Wdq": np.ascontiguousarray(np.asarray(L0_attn_W_dq, dtype=np.float32)),
        "Wuq": np.ascontiguousarray(np.asarray(L0_attn_W_uq, dtype=np.float32)),
        "Wdkv": np.ascontiguousarray(np.asarray(L0_attn_W_dkv, dtype=np.float32)),
        "Wuk": np.ascontiguousarray(np.asarray(L0_attn_W_uk, dtype=np.float32)),
        "Wuv": np.ascontiguousarray(np.asarray(L0_attn_W_uv, dtype=np.float32)),
        "Wo": np.ascontiguousarray(np.asarray(L0_attn_W_o, dtype=np.float32)),
        "WG": np.ascontiguousarray(np.asarray(L0_ffn_w_gate, dtype=np.float32)),
        "WU": np.ascontiguousarray(np.asarray(L0_ffn_w_up, dtype=np.float32)),
        "WD": np.ascontiguousarray(np.asarray(L0_ffn_w_down, dtype=np.float32)),
    }
    nc = build_nc()
    in_maps = []
    for core in range(8):
        m = {"X": np.ascontiguousarray(x[2 * core:2 * core + 2].reshape(TTOT, C))}
        m.update(w)
        in_maps.append(m)
    res = run_bass_kernel_spmd(nc, in_maps, core_ids=list(range(8)))
    out = np.empty((16, T, C), dtype=np.float32)
    for core in range(8):
        out[2 * core:2 * core + 2] = res.results[core]["OUT"].reshape(2, T, C)
    return out
